# version 04 (final)
# baseline (speedup 1.0000x reference)
import contextlib
import math
import numpy as np

import concourse.bass as bass
import concourse.mybir as mybir
from concourse.bass_utils import run_bass_kernel_spmd

F32 = mybir.dt.float32
BF16 = mybir.dt.bfloat16
I32 = mybir.dt.int32
AF = mybir.ActivationFunctionType
ALU = mybir.AluOpType
AXC = mybir.AxisListType

MAGIC = 12582912.0
B, N, D, H, DH, DFF = 2, 4096, 2048, 16, 128, 8192
NCH = D // 128
STRIP = 512
NSTRIP = N // STRIP
BLK = 256
TOWN = 1024
QK_SCALE = 1.0 / math.sqrt(128.0)
FFN1_SCALE = 1.0 / math.sqrt(8.0)

BUILD_STAGE = 5
TAPS = ()


def own_blocks(g):
    return sorted([g, 7 - g, 8 + g, 15 - g])


TRIPS = [(qs, ss, r) for qs in range(4) for ss in range(qs + 1)
         for r in range(4)]
NTRIP = len(TRIPS)
QS_BASE = [0, 4, 12, 24]
QS_N = [4, 8, 12, 16]


POS2OWN = {}
for _g in range(4):
    for _s, _b in enumerate(own_blocks(_g)):
        POS2OWN[_b] = (_g, _s)


class Ctr:
    def __init__(self, sem, step=1):
        self.sem = sem
        self.v = 0
        self.step = step

    def b(self, instr, n=None):
        n = self.step if n is None else n
        instr.then_inc(self.sem, n)
        self.v += n
        return self.v


class Arena:
    def __init__(self, t, cols):
        self.t, self.cols, self.ptr, self.marks = t, cols, 0, []

    def alloc(self, n):
        assert self.ptr + n <= self.cols, (self.ptr, n, self.cols)
        s = self.t[:, self.ptr:self.ptr + n]
        self.ptr += n
        return s

    def push(self):
        self.marks.append(self.ptr)

    def pop(self):
        self.ptr = self.marks.pop()


class Heap:
    def __init__(self, nc, handle, nbytes):
        self.nc, self.h, self.n = nc, handle, nbytes
        self.base = nc.lookup_mloc(handle).addr
        self.ptr, self.marks, self.cnt = 0, [], 0

    def alloc_t(self, shape, dtype):
        per = int(np.prod(shape[1:])) * mybir.dt.size(dtype)
        addr = (self.ptr + 31) & ~31
        assert addr + per <= self.n, ("heap overflow", addr, per, self.n)
        self.ptr = addr + per
        self.cnt += 1
        h = self.nc.virtual_sbuf_tensor(f"hp{self.cnt}", shape, dtype)
        self.nc._tensor(h.name, shape, dtype, type="SB")
        m = self.nc.lookup_mloc(h)
        m.allocated = True
        m.addr = self.base + addr
        return h

    def push(self):
        self.marks.append(self.ptr)

    def pop(self):
        self.ptr = self.marks.pop()


class TypedArena:
    def __init__(self, heap, dtype):
        self.heap, self.dt = heap, dtype

    def alloc(self, cols):
        return self.heap.alloc_t([128, cols], self.dt)[:, :]

    def push(self):
        self.heap.push()

    def pop(self):
        self.heap.pop()


def build_program(taps=()):
    nc = bass.Bass("TRN2", target_bir_lowering=False, debug=False)

    def ein(name, shape, dt=F32):
        return nc.dram_tensor(name, shape, dt, kind="ExternalInput").ap()

    xT = ein("xT", [D, N])
    xT_own = ein("xT_own", [D, TOWN])
    x_own_tm = ein("x_own_tm", [TOWN, D])
    wqkvT = ein("wqkvT", [NCH, 128, 1536])
    wgateT = ein("wgateT", [NCH, 128, D])
    woutT = ein("woutT", [NCH, 128, D])
    WqT = ein("WqT", [NCH, 128, D])
    WkT = ein("WkT", [NCH, 128, D])
    WvT = ein("WvT", [NCH, 128, D])
    WoT = ein("WoT", [NCH, 128, D])
    W1Ts = ein("W1Ts", [D, 1024])
    W2Ts = ein("W2Ts", [1024, D])
    factor = ein("factor", [D, D])
    slope4 = ein("slope4", [1, 4])
    bcols_h = ein("bcols_h", [128, 6 * NCH])
    bv = ein("bv", [1, D])
    fsel = ein("fsel", [1, NTRIP])
    fbias = ein("fbias", [1, NTRIP])

    y = nc.dram_tensor("y", [D, TOWN], F32, kind="ExternalOutput").ap()
    tap_t = {}
    for tname, tshape in taps:
        tap_t[tname] = nc.dram_tensor("tap_" + tname, tshape, F32,
                                      kind="ExternalOutput").ap()

    def idram(name, shape, dt=F32):
        return nc.dram_tensor(name, shape, dt, kind="Internal")

    ar_tiny_i = idram("ar_tiny_i", [1, 2])
    ar_tiny_o = idram("ar_tiny_o", [1, 2])
    agw_i = idram("agw_i", [4096, 1024], BF16)
    agw_o = idram("agw_o", [32768, 1024], BF16)
    arf_i = idram("arf_i", [D, D])
    arf_o = idram("arf_o", [D, D])
    aga_i = idram("aga_i", [512, N])
    x1d = idram("x1d", [D, TOWN])
    wg_b = idram("wg_b", [NCH, 128, D], BF16)
    wo_b = idram("wo_b", [NCH, 128, D], BF16)
    wq_b = idram("wq_b", [NCH, 128, D], BF16)
    wk_b = idram("wk_b", [NCH, 128, D], BF16)
    wv_b = idram("wv_b", [NCH, 128, D], BF16)
    wo1_b = idram("wo1_b", [NCH, 128, D], BF16)
    fs_b = idram("fs_b", [D, D], BF16)
    aga_o = idram("aga_o", [D, N])
    agkv_i = idram("agkv_i", [2048, 2048], BF16)
    agkv_o = idram("agkv_o", [8192, 2048], BF16)
    drep = idram("drep", [32, N])

    GRP4 = [[0, 1, 2, 3], [4, 5, 6, 7]]
    GRP8 = [list(range(8))]

    es = contextlib.ExitStack()
    with es:
        E = es.enter_context
        heap_t = E(nc.sbuf_tensor("heap", [128, 181200], mybir.dt.uint8))
        id_bf = E(nc.sbuf_tensor("id_bf", [128, 128], BF16))
        ones_bf = E(nc.sbuf_tensor("ones_bf", [128, 1], BF16))
        iscr = E(nc.sbuf_tensor("iscr", [128, 1024], I32))
        qdec = E(nc.sbuf_tensor("qdec", [128, 4 * BLK], BF16))
        kdec = E(nc.sbuf_tensor("kdec", [128, 8], F32))
        ddm = E(nc.sbuf_tensor("ddm", [128, 8 * BLK], BF16))
        bdec = E(nc.sbuf_tensor("bdec", [128, 4], F32))
        cmask = E(nc.sbuf_tensor("cmask", [128, 2 * BLK], BF16))
        srep = E(nc.sbuf_tensor("srep", [128, 8], F32))
        kvf = E(nc.sbuf_tensor("kvf", [128, 4 * 128], F32))
        kvb = E(nc.sbuf_tensor("kvb", [128, 4 * 128], BF16))
        bcols = E(nc.sbuf_tensor("bcols", [128, 6 * NCH], F32))
        bvrep = E(nc.sbuf_tensor("bvrep", [128, D], F32))
        rows = E(nc.sbuf_tensor("rows", [1, 4 * STRIP], F32))
        wrow = E(nc.sbuf_tensor("wrow", [128, 8], F32))
        cst = E(nc.sbuf_tensor("cst", [128, 4], F32))
        pb0 = E(nc.psum_tensor("pb0", [128, 512], F32))
        pb1 = E(nc.psum_tensor("pb1", [128, 512], F32))
        pb2 = E(nc.psum_tensor("pb2", [128, 512], F32))
        pb3 = E(nc.psum_tensor("pb3", [128, 512], F32))
        pb4 = E(nc.psum_tensor("pb4", [128, 512], F32))
        pb5 = E(nc.psum_tensor("pb5", [128, 512], F32))
        pt0 = E(nc.psum_tensor("pt0", [128, 1024], BF16))
        pt1 = E(nc.psum_tensor("pt1", [128, 1024], BF16))
        s_tq = E(nc.semaphore("s_tq"))
        s_aq = E(nc.semaphore("s_aq"))
        s_gq = E(nc.semaphore("s_gq"))
        s_cc = E(nc.semaphore("s_cc"))
        s_te = E(nc.semaphore("s_te"))
        s_ve = E(nc.semaphore("s_ve"))
        s_ae = E(nc.semaphore("s_ae"))
        s_ge = E(nc.semaphore("s_ge"))
        s_ds = E(nc.semaphore("s_ds"))
        s_w0 = E(nc.semaphore("s_w0"))
        s_w1 = E(nc.semaphore("s_w1"))
        s_w2 = E(nc.semaphore("s_w2"))
        s_w3 = E(nc.semaphore("s_w3"))
        s_x0 = E(nc.semaphore("s_x0"))
        s_x1 = E(nc.semaphore("s_x1"))
        s_gf = E(nc.semaphore("s_gf"))
        s_af = E(nc.semaphore("s_af"))
        block = E(nc.Block())
        pb = [pb0, pb1, pb2, pb3, pb4, pb5]
        TQ, AQ, GQ = Ctr(s_tq, 16), Ctr(s_aq, 16), Ctr(s_gq, 16)
        CC, TE, VE, AE, GE = Ctr(s_cc), Ctr(s_te), Ctr(s_ve), Ctr(s_ae), Ctr(s_ge)
        DS = Ctr(s_ds)
        GF = Ctr(s_gf)
        AFF = Ctr(s_af)
        SW = [Ctr(s_w0, 16), Ctr(s_w1, 16), Ctr(s_w2, 16), Ctr(s_w3, 16)]
        SWs = [s_w0, s_w1, s_w2, s_w3]
        SX = [Ctr(s_x0, 16), Ctr(s_x1, 16)]
        SXs = [s_x0, s_x1]
        heap = Heap(nc, heap_t, 181200)
        arF = TypedArena(heap, F32)
        arB = TypedArena(heap, BF16)
        st = {}
        arF.push()
        iscF = arF.alloc(1024)

        def p0_g(g):
            GE.b(g.iota(iscr[:, 0:128], pattern=[[-1, 128]], base=0,
                        channel_multiplier=1))
            GE.b(g.iota(iscr[:, 128:640], pattern=[[-128, 2], [1, BLK]],
                        base=0, channel_multiplier=-1))
            GE.b(g.iota(iscr[:, 640:896], pattern=[[1, BLK]], base=1,
                        channel_multiplier=0))
            GE.b(g.iota(iscr[:, 896:898], pattern=[[-128, 2]], base=255,
                        channel_multiplier=-1))
        block.gpsimd(p0_g)
        st['iota'] = GE.v

        def p0_s(sy):
            TQ.b(sy.dma_start(out=bcols[:, :], in_=bcols_h[:, :]))
            TQ.b(sy.dma_start(out=bvrep[:, :],
                              in_=bv[0:1, :].partition_broadcast(128)))
            TQ.b(sy.dma_start(out=srep[:, 0:4],
                              in_=slope4[0:1, :].partition_broadcast(128)))
        block.sync(p0_s)
        st['p0dma'] = TQ.v

        def p0_v(v):
            v.wait_ge(s_ge, st['iota'])
            VE.b(v.tensor_single_scalar(id_bf[:, :], iscr[:, 0:128], 0, ALU.is_equal))
            VE.b(v.memset(ones_bf[:, :], 1.0))
            VE.b(v.tensor_copy(iscF[:, 0:512], iscr[:, 128:640]))
            VE.b(v.tensor_copy(iscF[:, 512:768], iscr[:, 640:896]))
            VE.b(v.tensor_copy(iscF[:, 768:770], iscr[:, 896:898]))
            VE.b(v.tensor_single_scalar(cmask[:, :], iscr[:, 128:640], 0,
                                        ALU.is_ge))
            v.wait_ge(s_tq, st['p0dma'])
            VE.b(v.tensor_scalar_mul(srep[:, 4:8], srep[:, 0:4], -1.0))
            VE.b(v.memset(iscF[:, 770:771], 256.0))
            VE.b(v.memset(cst[:, 0:1], 1e-5))
            VE.b(v.memset(cst[:, 1:2], 1e-6))
            VE.b(v.memset(cst[:, 2:3], QK_SCALE))
            VE.b(v.memset(cst[:, 3:4], FFN1_SCALE))
        block.vector(p0_v)
        st['p0v'] = VE.v

        def p0_a(sc):
            sc.wait_ge(s_ve, st['p0v'])
            for i in range(4):
                ns = srep[:, 4 + i:5 + i]
                AE.b(sc.activation(qdec[:, i * BLK:(i + 1) * BLK],
                                   iscF[:, 512:768], AF.Exp, scale=ns))
                AE.b(sc.activation(kdec[:, 2 * i:2 * i + 2],
                                   iscF[:, 768:770], AF.Exp, scale=ns))
                AE.b(sc.activation(ddm[:, 2 * i * BLK:(2 * i + 2) * BLK],
                                   iscF[:, 0:512], AF.Exp, scale=ns))
                AE.b(sc.activation(bdec[:, i:i + 1], iscF[:, 770:771], AF.Exp,
                                   scale=ns))
        block.scalar(p0_a)
        st['p0a'] = AE.v

        def p0_v2(v):
            v.wait_ge(s_ae, st['p0a'])
            VE.b(v.tensor_single_scalar(iscF[:, 0:512], iscr[:, 128:640], 0,
                                        ALU.is_ge))
            v.wait_ge(s_ve, VE.v)
            for i in range(4):
                VE.b(v.tensor_mul(ddm[:, 2 * i * BLK:(2 * i + 2) * BLK],
                                  ddm[:, 2 * i * BLK:(2 * i + 2) * BLK],
                                  iscF[:, 0:512]))
        block.vector(p0_v2)
        st['p0v2'] = VE.v

        arF.pop()

        arF.push()
        arB.push()
        wqkv_sb = arB.alloc(NCH * 1536)
        qkvT = arB.alloc(12 * STRIP)
        xbs = arB.alloc(2 * STRIP)
        sqs = arB.alloc(2 * STRIP)
        ltB0 = arB.alloc(1280)
        ltB1 = arB.alloc(1280)
        xTs = arF.alloc(NCH * STRIP)
        tscr = arF.alloc(STRIP)
        mrep = arF.alloc(STRIP)
        ivrep = arF.alloc(STRIP)
        aoF = arF.alloc(8 * BLK)
        xnT = arB.alloc(NCH * STRIP)

        def p1_w(g):
            g.wait_ge(s_ve, st['p0v2'])
            for c in range(NCH):
                GQ.b(g.dma_start(out=wqkv_sb[:, c * 1536:(c + 1) * 1536],
                                 in_=wqkvT[c, :, :]))
        def p1_w_f(g):
            g.wait_ge(s_gq, GQ.v)
            g.sem_inc(s_gf, 1)
            GF.v += 1
        block.gpsimd(p1_w)
        block.gpsimd(p1_w_f)
        st['wqkv'] = GF.v

        def pcast(g):
            for dst, srcT in ((wg_b, wgateT), (wo_b, woutT), (wq_b, WqT),
                              (wk_b, WkT), (wv_b, WvT), (wo1_b, WoT)):
                GQ.b(g.dma_start(out=dst.ap()[:, :, :], in_=srcT[:, :, :]))
            g.wait_ge(s_gq, GQ.v)
            g.sem_inc(s_gf, 1)
            GF.v += 1
        block.gpsimd(pcast)
        st['pcast'] = GF.v

        if BUILD_STAGE >= 4:
            def f_all(g):
                GQ.b(g.dma_start(out=arf_i.ap()[:, :], in_=factor[:, :]))
                g.wait_ge(s_gq, GQ.v)
                CC.b(g.collective_compute(
                    "AllReduce", ALU.add, replica_groups=GRP8,
                    ins=[arf_i.ap().opt()], outs=[arf_o.ap().opt()]))
                g.wait_ge(s_cc, CC.v)
                GQ.b(g.dma_start(out=fs_b.ap()[:, :], in_=arf_o.ap()[:, :]))
                g.wait_ge(s_gq, GQ.v)
                g.sem_inc(s_gf, 1)
                GF.v += 1
            block.gpsimd(f_all)
            st['fs_cast'] = GF.v

        R_M, R_IV, R_S1, R_S2 = 0, STRIP, 2 * STRIP, 3 * STRIP

        for s in range(NSTRIP):
            t0 = s * STRIP

            def p1_dma(sy, s=s, t0=t0):
                if s == 0:
                    sy.wait_ge(s_ve, st['p0v2'])
                if s > 0:
                    sy.wait_ge(s_ve, st[('xn', s - 1)])
                for c in range(NCH):
                    TQ.b(sy.dma_start(out=xTs[:, c * STRIP:(c + 1) * STRIP],
                                      in_=xT[c * 128:(c + 1) * 128, t0:t0 + STRIP]))
            block.sync(p1_dma)
            st[('x', s)] = TQ.v

            def mk_cast(c):
                def f(v, s=s, c=c):
                    if c == 0:
                        v.wait_ge(s_tq, st[('x', s)])
                    k = ('statc', s, c - 2) if c >= 2 else (('statc', s - 1, 14 + c)
                                                            if s > 0 else None)
                    if k:
                        v.wait_ge(s_te, st[k])
                    VE.b(v.tensor_copy(xbs[:, (c % 2) * STRIP:(c % 2 + 1) * STRIP],
                                       xTs[:, c * STRIP:(c + 1) * STRIP]))
                    st[('xb', s, c)] = VE.v
                return f

            def mk_sq(c):
                def f(sc, s=s, c=c):
                    sc.wait_ge(s_ve, st[('xb', s, c)])
                    AE.b(sc.activation(sqs[:, (c % 2) * STRIP:(c % 2 + 1) * STRIP],
                                       xbs[:, (c % 2) * STRIP:(c % 2 + 1) * STRIP],
                                       AF.Square))
                    st[('sq', s, c)] = AE.v
                return f

            def mk_smm(c):
                def f(t, s=s, c=c):
                    if c == 0:
                        t.wait_ge(s_ve, st['p0v'])
                        if s > 0:
                            t.wait_ge(s_ve, st[('rowdone', s - 1)])
                    t.wait_ge(s_ae, st[('sq', s, c)])
                    x2 = xbs[:, (c % 2) * STRIP:(c % 2 + 1) * STRIP]
                    q2 = sqs[:, (c % 2) * STRIP:(c % 2 + 1) * STRIP]
                    for hh in range(2):
                        TE.b(t.matmul(pb5[0:1, hh * 256:hh * 256 + 256], ones_bf[:, :],
                                      x2[:, hh * 256:hh * 256 + 256],
                                      start=(c == 0 and hh == 0),
                                      stop=(c == NCH - 1 and hh == 1)))
                    for hh in range(2):
                        TE.b(t.matmul(pb4[0:1, hh * 256:hh * 256 + 256], ones_bf[:, :],
                                      q2[:, hh * 256:hh * 256 + 256],
                                      start=(c == 0 and hh == 0),
                                      stop=(c == NCH - 1 and hh == 1)))
                    st[('statc', s, c)] = TE.v
                    if c == NCH - 1:
                        st[('smm', s)] = TE.v
                return f

            for c in range(NCH):
                block.vector(mk_cast(c))
                block.scalar(mk_sq(c))
                block.tensor(mk_smm(c))

            def p1_rows(v, s=s, t0=t0):
                v.wait_ge(s_te, st[('smm', s)])
                if s > 0:
                    v.wait_ge(s_aq, st[('rep', s - 1)])
                    v.wait_ge(s_ae, st[('sd', s - 1)])
                    v.wait_ge(s_ve, st[('iv', s - 1)])
                VE.b(v.tensor_scalar_mul(rows[:, R_M:R_M + STRIP],
                                         pb5[0:1, :], 1.0 / D))
                VE.b(v.tensor_scalar_mul(rows[:, R_S1:R_S1 + STRIP],
                                         pb4[0:1, :], 1.0 / D))
                st[('rowdone', s)] = VE.v
                v.wait_ge(s_ve, VE.v)
                VE.b(v.tensor_mul(rows[:, R_S2:R_S2 + STRIP],
                                  rows[:, R_M:R_M + STRIP],
                                  rows[:, R_M:R_M + STRIP]))
                v.wait_ge(s_ve, VE.v)
                VE.b(v.tensor_sub(rows[:, R_S2:R_S2 + STRIP],
                                  rows[:, R_S1:R_S1 + STRIP],
                                  rows[:, R_S2:R_S2 + STRIP]))
            block.vector(p1_rows)
            st[('var', s)] = VE.v

            def p1_sd(sc, s=s):
                sc.wait_ge(s_ve, st[('var', s)])
                AE.b(sc.activation(rows[:, R_S2:R_S2 + STRIP],
                                   rows[:, R_S2:R_S2 + STRIP], AF.Sqrt, bias=cst[0:1, 0:1]))
            block.scalar(p1_sd)
            st[('sd', s)] = AE.v

            def p1_iv(v, s=s, t0=t0):
                v.wait_ge(s_ae, st[('sd', s)])
                VE.b(v.reciprocal(rows[:, R_IV:R_IV + STRIP],
                                  rows[:, R_S2:R_S2 + STRIP]))
            block.vector(p1_iv)
            st[('iv', s)] = VE.v

            def p1_rep(sc, s=s, t0=t0):
                sc.wait_ge(s_ve, st[('iv', s)])
                if s > 0:
                    sc.wait_ge(s_ve, st[('xn', s - 1)])
                AQ.b(sc.dma_start(out=drep.ap()[0:1, t0:t0 + STRIP],
                                  in_=rows[:, R_M:R_M + STRIP]))
                AQ.b(sc.dma_start(out=drep.ap()[1:2, t0:t0 + STRIP],
                                  in_=rows[:, R_IV:R_IV + STRIP]))
                sc.wait_ge(s_aq, AQ.v)
                AQ.b(sc.dma_start(out=mrep[:, :],
                                  in_=drep.ap()[0:1, t0:t0 + STRIP].partition_broadcast(128)))
                AQ.b(sc.dma_start(out=ivrep[:, :],
                                  in_=drep.ap()[1:2, t0:t0 + STRIP].partition_broadcast(128)))
            block.scalar(p1_rep)
            st[('rep', s)] = AQ.v

            def p1_xn(v, s=s):
                v.wait_ge(s_aq, st[('rep', s)])
                if s > 0:
                    v.wait_ge(s_te, st[('qkvmm', s - 1)])
                for c in range(NCH):
                    v.wait_ge(s_ve, VE.v)
                    VE.b(v.tensor_sub(tscr[:, :], xTs[:, c * STRIP:(c + 1) * STRIP],
                                      mrep[:, :]))
                    v.wait_ge(s_ve, VE.v)
                    VE.b(v.tensor_mul(xnT[:, c * STRIP:(c + 1) * STRIP],
                                      tscr[:, :], ivrep[:, :]))
            block.vector(p1_xn)
            st[('xn', s)] = VE.v

            for of in range(12):
                def p1_qkv(t, s=s, of=of):
                    if of == 0:
                        t.wait_ge(s_ve, st[('xn', s)])
                        t.wait_ge(s_gf, st['wqkv'])
                        if ('ltev_last', s - 1) in st:
                            t.wait_ge(s_ve, st[('ltev_last', s - 1)])
                    k = ('silu', s, of - 3) if of >= 3 else (('silu', s - 1, of + 9)
                                                             if s > 0 else None)
                    if k and k in st:
                        t.wait_ge(s_ae, st[k])
                    bank = pb[of % 3]
                    for c in range(NCH):
                        TE.b(t.matmul(
                            bank[:, :],
                            wqkv_sb[:, c * 1536 + of * 128:c * 1536 + (of + 1) * 128],
                            xnT[:, c * STRIP:(c + 1) * STRIP],
                            start=(c == 0), stop=(c == NCH - 1)))
                    st[('qkvof', s, of)] = TE.v
                block.tensor(p1_qkv)
                if of == 11:
                    st[('qkvmm', s)] = TE.v

                def p1_silu(sc, s=s, of=of):
                    sc.wait_ge(s_te, st[('qkvof', s, of)])
                    if of == 0 and ('ltlast', s - 1) in st:
                        sc.wait_ge(s_te, st[('ltlast', s - 1)])
                        sc.wait_ge(s_ve, st[('ltev_last', s - 1)])
                    AE.b(sc.activation(qkvT[:, of * STRIP:(of + 1) * STRIP],
                                       pb[of % 3][:, :], AF.Silu))
                    st[('silu', s, of)] = AE.v
                block.scalar(p1_silu)

            for bi in range(2):
                blk = 2 * s + bi
                bc = bi * BLK
                for hi in range(4):
                    idx = blk * 4 + hi
                    ltB = ltB0 if idx % 2 == 0 else ltB1
                    ptb = pt0 if idx % 2 == 0 else pt1
                    QdT = ltB[:, 0:BLK]
                    PT0 = ltB[:, BLK:2 * BLK]
                    PT1 = ltB[:, 2 * BLK:3 * BLK]
                    Kd0 = ltB[:, 3 * BLK:3 * BLK + 128]
                    Kd1 = ltB[:, 3 * BLK + 128:4 * BLK]
                    Vt0 = ltB[:, 4 * BLK:4 * BLK + 128]
                    Vt1 = ltB[:, 4 * BLK + 128:5 * BLK]
                    qof, kof, vof = hi, 4 + hi, 8 + hi
                    Qs = qkvT[:, qof * STRIP + bc:qof * STRIP + bc + BLK]
                    Ks = qkvT[:, kof * STRIP + bc:kof * STRIP + bc + BLK]
                    Vs = qkvT[:, vof * STRIP + bc:vof * STRIP + bc + BLK]
                    kv_f = kvf[:, hi * 128:(hi + 1) * 128]
                    kv_b = kvb[:, hi * 128:(hi + 1) * 128]
                    need = st[('silu', s, vof)]

                    def lt_qd(v, Qs=Qs, QdT=QdT, hi=hi, need=need, idx=idx):
                        v.wait_ge(s_ae, need)
                        if ('ltmm', idx - 2) in st:
                            v.wait_ge(s_te, st[('ltmm', idx - 2)])
                        v.wait_ge(s_ve, st['p0v2'])
                        VE.b(v.tensor_mul(QdT, Qs, qdec[:, hi * BLK:(hi + 1) * BLK]))
                        st[('ltqd', idx)] = VE.v
                    block.vector(lt_qd)

                    def lt_tr(t, Ks=Ks, Vs=Vs, need=need, idx=idx, ptb=ptb):
                        t.wait_ge(s_ae, need)
                        if ('lttrc', idx - 2) in st:
                            t.wait_ge(s_ve, st[('lttrc', idx - 2)])
                        for h2 in range(2):
                            TE.b(t.transpose(ptb[:, h2 * 128:(h2 + 1) * 128],
                                             Ks[:, h2 * 128:(h2 + 1) * 128], id_bf[:, :]))
                            TE.b(t.transpose(ptb[:, 256 + h2 * 128:256 + (h2 + 1) * 128],
                                             Vs[:, h2 * 128:(h2 + 1) * 128], id_bf[:, :]))
                        st[('lttr', idx)] = TE.v
                    block.tensor(lt_tr)

                    def lt_trev(v, idx=idx, hi=hi, Kd0=Kd0, Kd1=Kd1, Vt0=Vt0,
                                Vt1=Vt1, ptb=ptb):
                        v.wait_ge(s_te, st[('lttr', idx)])
                        if ('ltmm', idx - 2) in st:
                            v.wait_ge(s_te, st[('ltmm', idx - 2)])
                        VE.b(v.tensor_scalar_mul(Kd0, ptb[:, 0:128],
                                                 kdec[:, 2 * hi:2 * hi + 1]))
                        VE.b(v.tensor_scalar_mul(Kd1, ptb[:, 128:256],
                                                 kdec[:, 2 * hi + 1:2 * hi + 2]))
                        VE.b(v.tensor_copy(Vt0, ptb[:, 256:384]))
                        VE.b(v.tensor_copy(Vt1, ptb[:, 384:512]))
                        st[('lttrc', idx)] = VE.v
                    block.vector(lt_trev)

                    def lt_st(t, Ks=Ks, Qs=Qs, idx=idx):
                        if ('ltpt', idx - 1) in st:
                            t.wait_ge(s_ve, st[('ltpt', idx - 1)])
                        for h2 in range(2):
                            TE.b(t.matmul(pb3[:, h2 * BLK:(h2 + 1) * BLK],
                                          Ks[:, h2 * 128:(h2 + 1) * 128], Qs,
                                          start=True, stop=True))
                        st[('ltst', idx)] = TE.v
                    block.tensor(lt_st)

                    def lt_pt(v, idx=idx, hi=hi, PT0=PT0, PT1=PT1):
                        v.wait_ge(s_te, st[('ltst', idx)])
                        if ('ltmm', idx - 2) in st:
                            v.wait_ge(s_te, st[('ltmm', idx - 2)])
                        VE.b(v.tensor_mul(PT0, pb3[:, 0:BLK],
                                          ddm[:, 2 * hi * BLK:(2 * hi + 1) * BLK]))
                        VE.b(v.tensor_mul(PT1, pb3[:, BLK:2 * BLK],
                                          ddm[:, (2 * hi + 1) * BLK:(2 * hi + 2) * BLK]))
                        st[('ltpt', idx)] = VE.v
                    block.vector(lt_pt)

                    def lt_mm(t, idx=idx, blk=blk, s=s, QdT=QdT, PT0=PT0, PT1=PT1,
                              Kd0=Kd0, Kd1=Kd1, Vt0=Vt0, Vt1=Vt1, kv_b=kv_b):
                        t.wait_ge(s_ae, st[('silu', s, 11)])
                        t.wait_ge(s_ve, st[('ltpt', idx)])
                        t.wait_ge(s_ve, st[('ltqd', idx)])
                        t.wait_ge(s_ve, st[('lttrc', idx)])
                        if ('ltev', idx - 1) in st:
                            t.wait_ge(s_ve, st[('ltev', idx - 1)])
                        first = (blk == 0)
                        if not first:
                            TE.b(t.matmul(pb2[:, 0:BLK], kv_b, QdT,
                                          start=True, stop=False))
                        TE.b(t.matmul(pb2[:, 0:BLK], Vt0, PT0,
                                      start=first, stop=False))
                        TE.b(t.matmul(pb2[:, 0:BLK], Vt1, PT1,
                                      start=False, stop=True))
                        TE.b(t.matmul(pb2[:, BLK:BLK + 128], Kd0, Vt0,
                                      start=True, stop=False))
                        TE.b(t.matmul(pb2[:, BLK:BLK + 128], Kd1, Vt1,
                                      start=False, stop=True))
                        st[('ltmm', idx)] = TE.v
                    block.tensor(lt_mm)

                    def lt_ev(v, idx=idx, blk=blk, hi=hi, kv_f=kv_f, kv_b=kv_b):
                        v.wait_ge(s_te, st[('ltmm', idx)])
                        if ('aofence', s - 1) in st:
                            v.wait_ge(s_ds, st[('aofence', s - 1)])
                        ao = aoF[:, (idx % 8) * BLK:(idx % 8 + 1) * BLK]
                        VE.b(v.tensor_copy(ao, pb2[:, 0:BLK]))
                        st[('ltao', idx)] = VE.v
                        if blk == 0:
                            VE.b(v.tensor_copy(kv_f, pb2[:, BLK:BLK + 128]))
                        else:
                            VE.b(v.scalar_tensor_tensor(kv_f, kv_f, bdec[:, hi:hi + 1],
                                                        pb2[:, BLK:BLK + 128],
                                                        ALU.mult, ALU.add))
                        v.wait_ge(s_ve, VE.v)
                        VE.b(v.tensor_copy(kv_b, kv_f))
                        st[('ltev', idx)] = VE.v
                    block.vector(lt_ev)

                    def lt_dma(sy, idx=idx, blk=blk, hi=hi):
                        sy.wait_ge(s_ve, st[('ltao', idx)])
                        ao = aoF[:, (idx % 8) * BLK:(idx % 8 + 1) * BLK]
                        oo, ss = POS2OWN[blk]
                        TQ.b(sy.dma_start(
                            out=aga_i.ap()[hi * 128:(hi + 1) * 128,
                                           oo * TOWN + ss * BLK:
                                           oo * TOWN + ss * BLK + BLK],
                            in_=ao))
                        st[('ltdma', idx)] = TQ.v
                    block.sync(lt_dma)

            st[('ltlast', s)] = TE.v
            st[('ltev_last', s)] = VE.v

            def p1_fence(sy, s=s):
                sy.wait_ge(s_tq, TQ.v)
                sy.sem_inc(s_ds, 1)
                DS.v += 1
            block.sync(p1_fence)
            st[('aofence', s)] = DS.v

        st['p1_tq'] = TQ.v

        if BUILD_STAGE >= 3:
            arF.push()
            arB.push()
            wt = [arF.alloc(2048), arF.alloc(2048)]
            wres = arF.alloc(2048)
            wqb = [arB.alloc(2048), arB.alloc(2048)]
            pieces = [(0, pi) for pi in range(8)] + [(1, pi) for pi in range(8)]
            CW = (0.01763, 0.008815)

            def w_in_ap(half, pi):
                if half == 0:
                    return W1Ts[pi * 256:(pi + 1) * 256, :].rearrange(
                        "(c p) f -> p c f", p=128)
                return W2Ts[pi * 128:(pi + 1) * 128, :]

            def wA_init(v):
                VE.b(v.memset(wrow[:, 0:2], 0.0))
            block.vector(wA_init)
            st['wac0'] = VE.v

            for i, (half, pi) in enumerate(pieces):
                def wA_ld(g, i=i, half=half, pi=pi):
                    if i >= 2:
                        g.wait_ge(s_ve, st[('wAr', i - 2)])
                    GQ.b(g.dma_start(out=wt[i % 2][:, :], in_=w_in_ap(half, pi)))
                    g.wait_ge(s_gq, GQ.v)
                    g.sem_inc(s_gf, 1)
                    GF.v += 1
                    st[('wAf', i)] = GF.v
                block.gpsimd(wA_ld)

                def wA_abs(sc, i=i):
                    sc.wait_ge(s_gf, st[('wAf', i)])
                    if i > 0:
                        sc.wait_ge(s_ve, st[('wAr', i - 1)])
                    AE.b(sc.activation(wres[:, :], wt[i % 2][:, :], AF.Abs))
                    st[('wAb', i)] = AE.v
                block.scalar(wA_abs)

                def wA_red(v, i=i, half=half):
                    v.wait_ge(s_ae, st[('wAb', i)])
                    v.wait_ge(s_ve, st['wac0'] if i == 0 else st[('wAr', i - 1)])
                    VE.b(v.tensor_scalar_add(wres[:, :], wres[:, :], -CW[half]))
                    v.wait_ge(s_ve, VE.v)
                    VE.b(v.reduce_sum(wrow[:, 4:5], wres[:, :], axis=AXC.X))
                    v.wait_ge(s_ve, VE.v)
                    VE.b(v.tensor_add(wrow[:, half:half + 1],
                                      wrow[:, half:half + 1], wrow[:, 4:5]))
                    st[('wAr', i)] = VE.v
                block.vector(wA_red)

            def w_redc(g):
                g.wait_ge(s_ve, st[('wAr', 15)])
                GE.b(g.tensor_reduce(wrow[0:1, 2:4], wrow[:, 0:2], axis=AXC.C,
                                     op=ALU.add))
            block.gpsimd(w_redc)
            st['wredc'] = GE.v

            def w_ar(g):
                g.wait_ge(s_ge, st['wredc'])
                GQ.b(g.dma_start(out=ar_tiny_i.ap()[0:1, :], in_=wrow[0:1, 2:4]))
                g.wait_ge(s_gq, GQ.v)
                CC.b(g.collective_compute(
                    "AllReduce", ALU.add, replica_groups=GRP8,
                    ins=[ar_tiny_i.ap().opt()], outs=[ar_tiny_o.ap().opt()]))
                g.wait_ge(s_cc, CC.v)
                GQ.b(g.dma_start(out=wrow[:, 6:8],
                                 in_=ar_tiny_o.ap()[0:1, :].partition_broadcast(128)))
                g.wait_ge(s_gq, GQ.v)
                g.sem_inc(s_gf, 1)
                GF.v += 1
            block.gpsimd(w_ar)
            st['w_ar'] = GF.v

            def w_sc(v):
                v.wait_ge(s_gf, st['w_ar'])
                VE.b(v.tensor_scalar(wrow[:, 0:1], wrow[:, 6:7],
                                     1.0 / (D * DFF), CW[0], ALU.mult, ALU.add))
                VE.b(v.tensor_scalar(wrow[:, 1:2], wrow[:, 7:8],
                                     1.0 / (D * DFF), CW[1], ALU.mult, ALU.add))
                v.wait_ge(s_ve, VE.v)
                VE.b(v.tensor_scalar_max(wrow[:, 0:2], wrow[:, 0:2], 1e-5))
                v.wait_ge(s_ve, VE.v)
                VE.b(v.reciprocal(wrow[:, 2:4], wrow[:, 0:2]))
            block.vector(w_sc)
            st['w_sc'] = VE.v

            if 'wsc' in tap_t:
                def tap_wsc(sy):
                    sy.wait_ge(s_ve, st['w_sc'])
                    TQ.b(sy.dma_start(out=tap_t['wsc'][:, :],
                                      in_=wrow[0:1, 0:8]))
                    sy.wait_ge(s_tq, TQ.v)
                block.sync(tap_wsc)

            for i, (half, pi) in enumerate(pieces):
                def wB_ld(g, i=i, half=half, pi=pi):
                    if i < 2:
                        g.wait_ge(s_ve, st[('wAr', 15)])
                    else:
                        g.wait_ge(s_ve, st[('wBq', i - 2)])
                    GQ.b(g.dma_start(out=wt[i % 2][:, :], in_=w_in_ap(half, pi)))
                    g.wait_ge(s_gq, GQ.v)
                    g.sem_inc(s_gf, 1)
                    GF.v += 1
                    st[('wBf', i)] = GF.v
                block.gpsimd(wB_ld)

                def wB_q(v, i=i, half=half):
                    v.wait_ge(s_gf, st[('wBf', i)])
                    v.wait_ge(s_ve, st['w_sc'] if i == 0 else st[('wBq', i - 1)])
                    if i >= 2:
                        v.wait_ge(s_gf, st[('wBsf', i - 2)])
                    w = wt[i % 2][:, :]
                    VE.b(v.tensor_scalar(w, w, wrow[:, 2 + half:3 + half],
                                         MAGIC, ALU.mult, ALU.add))
                    v.wait_ge(s_ve, VE.v)
                    VE.b(v.tensor_scalar(w, w, MAGIC, 1.0, ALU.subtract, ALU.min))
                    v.wait_ge(s_ve, VE.v)
                    VE.b(v.tensor_scalar_max(wqb[i % 2][:, :], w, -1.0))
                    st[('wBq', i)] = VE.v
                block.vector(wB_q)

                def wB_st(g, i=i, half=half, pi=pi):
                    g.wait_ge(s_ve, st[('wBq', i)])
                    if half == 0:
                        out = agw_i.ap()[pi * 256:(pi + 1) * 256, :].rearrange(
                            "(c p) f -> p c f", p=128)
                    else:
                        out = agw_i.ap()[2048 + pi * 256:2048 + (pi + 1) * 256, :]
                    GQ.b(g.dma_start(out=out, in_=wqb[i % 2][:, :]))
                    g.wait_ge(s_gq, GQ.v)
                    g.sem_inc(s_gf, 1)
                    GF.v += 1
                    st[('wBsf', i)] = GF.v
                block.gpsimd(wB_st)

            def w_ag(g):
                g.wait_ge(s_gf, st[('wBsf', 15)])
                CC.b(g.collective_compute(
                    "AllGather", ALU.bypass, replica_groups=GRP8,
                    ins=[agw_i.ap().opt()], outs=[agw_o.ap().opt()]))
            block.gpsimd(w_ag)
            st['ag_w'] = CC.v
            arF.pop()
            arB.pop()

        if BUILD_STAGE >= 2:
            def p2(g):
                g.wait_ge(s_tq, st['p1_tq'])
                CC.b(g.collective_compute(
                    "AllGather", ALU.bypass, replica_groups=GRP4,
                    ins=[aga_i.ap().opt()], outs=[aga_o.ap().opt()]))
            block.gpsimd(p2)
            st['ag_a0'] = CC.v

        if 'attn0' in tap_t:
            def tap_a0(sy):
                sy.wait_ge(s_tq, st['p1_tq'])
                TQ.b(sy.dma_start(out=tap_t['attn0'][:, :], in_=aga_i.ap()[:, :]))
                sy.wait_ge(s_tq, TQ.v)
            block.sync(tap_a0)

        if 'xn' in tap_t:
            def tap_xn(v):
                v.wait_ge(s_te, st[('qkvmm', NSTRIP - 1)])
                for c in range(NCH):
                    VE.b(v.tensor_copy(xTs[:, c * STRIP:(c + 1) * STRIP],
                                       xnT[:, c * STRIP:(c + 1) * STRIP]))
            block.vector(tap_xn)

            def tap_xn2(sy):
                sy.wait_ge(s_ve, VE.v)
                for c in range(NCH):
                    TQ.b(sy.dma_start(out=tap_t['xn'][c * 128:(c + 1) * 128, :],
                                      in_=xTs[:, c * STRIP:(c + 1) * STRIP]))
                sy.wait_ge(s_tq, TQ.v)
            block.sync(tap_xn2)

        if 'qkv' in tap_t:
            qkvF = arF.alloc(STRIP)

            def tap_qkv(sy):
                sy.wait_ge(s_tq, st['p1_tq'])
                pass
            def tap_qkv_v(v):
                v.wait_ge(s_ae, st[('silu', NSTRIP - 1, 11)])
                for of in range(12):
                    VE.b(v.tensor_copy(qkvF[:, :], qkvT[:, of * STRIP:(of + 1) * STRIP]))
                    st[('tapq', of)] = VE.v
                    v.wait_ge(s_tq, TQ.v + of * 16)
            def tap_qkv_serial(v):
                v.wait_ge(s_ae, st[('silu', NSTRIP - 1, 11)])
            block.vector(tap_qkv_serial)
            for of in range(12):
                def tq_v(v, of=of):
                    if of > 0:
                        v.wait_ge(s_tq, st[('tapqd', of - 1)])
                    VE.b(v.tensor_copy(qkvF[:, :], qkvT[:, of * STRIP:(of + 1) * STRIP]))
                    st[('tapq', of)] = VE.v
                block.vector(tq_v)

                def tq_s(sy, of=of):
                    sy.wait_ge(s_ve, st[('tapq', of)])
                    TQ.b(sy.dma_start(out=tap_t['qkv'][of * 128:(of + 1) * 128, :],
                                      in_=qkvF[:, :]))
                    st[('tapqd', of)] = TQ.v
                block.sync(tq_s)

        if BUILD_STAGE >= 3:
            arF.pop()
            arB.pop()
            arF.push()
            arB.push()
            stc = arF.alloc(16)
            xtm = arF.alloc(2048)
            xto = [arF.alloc(512), arF.alloc(512)]
            scrF = [arF.alloc(512), arF.alloc(512)]
            acc = arF.alloc(512)
            accn = arF.alloc(512)
            at0 = [arF.alloc(512), arF.alloc(512)]
            m_rp = arF.alloc(512)
            iv_rp = arF.alloc(512)
            A_rp = arF.alloc(512)
            B_rp = arF.alloc(512)
            i1_rp = arF.alloc(512)
            s2_rp = arF.alloc(512)
            i2_rp = arF.alloc(512)
            r_rp = arF.alloc(512)
            x1h = arF.alloc(NCH * 512)
            sqb2 = arB.alloc(2 * 512)

            ent_te = st[('ltlast', NSTRIP - 1)]
            ent_ve = max(st[('ltev_last', NSTRIP - 1)], st[('wBq', 15)])
            ent_ae = st[('silu', NSTRIP - 1, 11)]
            ent_ds = st[('aofence', NSTRIP - 1)]
            ent_gf = st[('wBsf', 15)]
            ent_tap = st.get('tap_tq', 0)

            def p3_guard(eng):
                eng.wait_ge(s_te, ent_te)
                eng.wait_ge(s_ve, ent_ve)
                eng.wait_ge(s_ae, ent_ae)
                eng.wait_ge(s_ds, ent_ds)
                eng.wait_ge(s_gf, ent_gf)
                if ent_tap:
                    eng.wait_ge(s_tq, ent_tap)

            REG = {}

            def mk_reg(eng, nm):
                r = eng.alloc_register(nm)
                pid = eng.partition_id()
                g4 = eng.compute_val(pid % 4)
                eng.reg_mov(r, g4)
                REG[nm] = r
            block.sync(lambda sy: mk_reg(sy, 'g_sy'))

            def tqs(sy, out, in_, par, guards=()):
                for sem_, val in guards:
                    sy.wait_ge(sem_, val)
                SX[par].b(sy.dma_start(out=out, in_=in_))
                return SX[par].v

            def tqs_br(sy, out, src_fn, par, guards=()):
                for sem_, val in guards:
                    sy.wait_ge(sem_, val)
                for gg in range(4):
                    with sy.If_eq(REG['g_sy'], gg):
                        sy.dma_start(out=out, in_=src_fn(gg)).then_inc(SXs[par].sem
                                     if False else SX[par].sem, 16)
                SX[par].v += 16
                return SX[par].v

            for th in range(2):
                t0 = th * 512
                for i in range(4):
                    ti = th * 4 + i

                    def a_dma(sy, ti=ti, i=i):
                        if ti == 0:
                            p3_guard(sy)
                        g = []
                        if ('a_st', ti - 1) in st:
                            g.append((s_ve, st[('a_st', ti - 1)]))
                        if i < 2 and ('x1e', 0, 14 + i) in st:
                            g.append((s_ve, st[('x1e', 0, 14 + i)]))
                        st[('a_ld', ti)] = tqs(
                            sy, xtm[:, :], x_own_tm[ti * 128:(ti + 1) * 128, :],
                            ti % 2, g)
                    block.sync(a_dma)

                    def a_red(v, ti=ti):
                        if ti == 0:
                            p3_guard(v)
                        v.wait_ge(SXs[ti % 2], st[('a_ld', ti)])
                        if ('a_st', ti - 1) in st:
                            v.wait_ge(s_ve, st[('a_st', ti - 1)])
                        VE.b(v.reduce_sum(stc[:, 0:1], xtm[:, :], axis=AXC.X))
                        VE.b(v.reduce_max(stc[:, 2:3], xtm[:, :], axis=AXC.X))
                        VE.b(v.tensor_reduce(stc[:, 3:4], xtm[:, :], axis=AXC.X,
                                             op=ALU.min))
                        st[('a_red', ti)] = VE.v
                    block.vector(a_red)

                    def a_sq(sc, ti=ti):
                        if ti == 0:
                            p3_guard(sc)
                        sc.wait_ge(s_ve, st[('a_red', ti)])
                        AE.b(sc.activation(xtm[:, :], xtm[:, :], AF.Square,
                                           accum_out=stc[:, 1:2]))
                        st[('a_sq', ti)] = AE.v
                    block.scalar(a_sq)

                    def a_st(v, ti=ti):
                        v.wait_ge(s_ae, st[('a_sq', ti)])
                        if ('a_bnc', ti - 1) in st:
                            v.wait_ge(s_aq, st[('a_bnc', ti - 1)])
                        VE.b(v.tensor_scalar_mul(stc[:, 4:5], stc[:, 0:1], 1.0 / D))
                        VE.b(v.tensor_scalar_mul(stc[:, 5:6], stc[:, 1:2], 1.0 / D))
                        v.wait_ge(s_ve, VE.v)
                        VE.b(v.tensor_mul(stc[:, 6:7], stc[:, 4:5], stc[:, 4:5]))
                        v.wait_ge(s_ve, VE.v)
                        VE.b(v.tensor_sub(stc[:, 6:7], stc[:, 5:6], stc[:, 6:7]))
                        st[('a_var', ti)] = VE.v
                    block.vector(a_st)

                    def a_sd(sc, ti=ti):
                        sc.wait_ge(s_ve, st[('a_var', ti)])
                        AE.b(sc.activation(stc[:, 6:7], stc[:, 6:7], AF.Sqrt,
                                           bias=cst[:, 0:1]))
                        st[('a_sd', ti)] = AE.v
                    block.scalar(a_sd)

                    def a_fin(v, ti=ti):
                        v.wait_ge(s_ae, st[('a_sd', ti)])
                        if ('a_bnc', ti - 1) in st:
                            v.wait_ge(s_aq, st[('a_bnc', ti - 1)])
                        VE.b(v.reciprocal(stc[:, 7:8], stc[:, 6:7]))
                        v.wait_ge(s_ve, VE.v)
                        VE.b(v.tensor_sub(stc[:, 8:9], stc[:, 2:3], stc[:, 4:5]))
                        VE.b(v.tensor_sub(stc[:, 9:10], stc[:, 4:5], stc[:, 3:4]))
                        v.wait_ge(s_ve, VE.v)
                        VE.b(v.tensor_max(stc[:, 8:9], stc[:, 8:9], stc[:, 9:10]))
                        v.wait_ge(s_ve, VE.v)
                        VE.b(v.tensor_mul(stc[:, 8:9], stc[:, 8:9], stc[:, 7:8]))
                        v.wait_ge(s_ve, VE.v)
                        VE.b(v.tensor_scalar_max(stc[:, 8:9], stc[:, 8:9], 1e-5))
                        v.wait_ge(s_ve, VE.v)
                        VE.b(v.reciprocal(stc[:, 9:10], stc[:, 8:9]))
                        v.wait_ge(s_ve, VE.v)
                        VE.b(v.tensor_scalar_mul(stc[:, 9:10], stc[:, 9:10], 127.0))
                        v.wait_ge(s_ve, VE.v)
                        VE.b(v.tensor_mul(stc[:, 10:11], stc[:, 7:8], stc[:, 9:10]))
                        v.wait_ge(s_ve, VE.v)
                        VE.b(v.tensor_mul(stc[:, 11:12], stc[:, 4:5], stc[:, 10:11]))
                        v.wait_ge(s_ve, VE.v)
                        VE.b(v.tensor_scalar_mul(stc[:, 11:12], stc[:, 11:12], -1.0))
                        VE.b(v.tensor_scalar(stc[:, 12:13], stc[:, 8:9],
                                             wrow[:, 0:1], 1.0 / 127.0,
                                             ALU.mult, ALU.mult))
                        st[('a_fin', ti)] = VE.v
                        st[('a_st', ti)] = VE.v
                    block.vector(a_fin)

                    def a_bnc(sc, ti=ti, i=i, th=th):
                        sc.wait_ge(s_ve, st[('a_fin', ti)])
                        c0 = th * 512 + i * 128
                        for j, col in ((2, 4), (3, 7), (4, 10), (5, 11), (6, 12)):
                            AQ.b(sc.dma_start(out=drep.ap()[j:j + 1, c0:c0 + 128],
                                              in_=stc[:, col:col + 1]))
                        st[('a_bnc', ti)] = AQ.v
                    block.scalar(a_bnc)

                def a_rep(sc, th=th, t0=t0):
                    sc.wait_ge(s_aq, AQ.v)
                    if th > 0:
                        sc.wait_ge(s_ve, st[('b_xq', 0, NCH - 1)])
                    for rp, j in ((m_rp, 2), (iv_rp, 3), (A_rp, 4), (B_rp, 5),
                                  (i1_rp, 6)):
                        AQ.b(sc.dma_start(
                            out=rp[:, :],
                            in_=drep.ap()[j:j + 1, t0:t0 + 512].partition_broadcast(128)))
                    sc.wait_ge(s_aq, AQ.v)
                    sc.sem_inc(s_af, 1)
                    AFF.v += 1
                    st[('a_rep', th)] = AFF.v
                block.scalar(a_rep)

                arB.push()
                xqT = arB.alloc(NCH * 512)
                arB.push()
                xnoB = arB.alloc(NCH * 512)
                gateT = arB.alloc(NCH * 512)
                wjit = arB.alloc(4 * 512)

                for c in range(NCH):
                    def b_dma(sy, c=c, th=th, t0=t0):
                        g = []
                        if ('b_xq', th, c - 2) in st:
                            g.append((s_ve, st[('b_xq', th, c - 2)]))
                        if th > 0 and c < 2:
                            g.append((s_ve, st[('b_xq', 0, NCH - 2 + c)]))
                        if c < 2:
                            g.append((s_ve, st[('a_st', th * 4 + 3)]))
                        st[('b_ld', th, c)] = tqs(
                            sy, xto[c % 2][:, :],
                            xT_own[c * 128:(c + 1) * 128, t0:t0 + 512], c % 2, g)
                    block.sync(b_dma)

                    def b_v(v, c=c, th=th):
                        v.wait_ge(SXs[c % 2], st[('b_ld', th, c)])
                        v.wait_ge(s_af, st[('a_rep', th)])
                        if ('b_xq', th, c - 2) in st:
                            v.wait_ge(s_ve, st[('b_xq', th, c - 2)])
                        xx = xto[c % 2][:, :]
                        s1 = scrF[c % 2][:, :]
                        VE.b(v.tensor_sub(s1, xx, m_rp[:, :]))
                        v.wait_ge(s_ve, VE.v)
                        VE.b(v.tensor_mul(xnoB[:, c * 512:(c + 1) * 512], s1,
                                          iv_rp[:, :]))
                        v.wait_ge(s_ve, VE.v)
                        VE.b(v.tensor_mul(s1, xx, A_rp[:, :]))
                        v.wait_ge(s_ve, VE.v)
                        VE.b(v.tensor_add(s1, s1, B_rp[:, :]))
                        v.wait_ge(s_ve, VE.v)
                        VE.b(v.tensor_scalar(s1, s1, MAGIC, MAGIC, ALU.add,
                                             ALU.subtract))
                        v.wait_ge(s_ve, VE.v)
                        VE.b(v.tensor_scalar(xqT[:, c * 512:(c + 1) * 512], s1,
                                             127.0, -128.0, ALU.min, ALU.max))
                        st[('b_xq', th, c)] = VE.v
                    block.vector(b_v)
                st[('b_done', th)] = VE.v

                def jit_mm(tag, wsrc_fn, nGrp, nChunk, rhs_fn, evac_fn,
                           mm_wait=None, ld_guards=(), th=th):
                    for grp in range(nGrp):
                        for ch in range(nChunk):
                            slot = (grp * nChunk + ch) % 4

                            def w_ld(sy, grp=grp, ch=ch, slot=slot, tag=tag):
                                k = (tag, th, 'mm', grp, ch - 4) if ch >= 4 else \
                                    ((tag, th, 'mm', grp - 1, nChunk - 4 + ch)
                                     if grp > 0 else None)
                                if k and k in st:
                                    sy.wait_ge(s_te, st[k])
                                else:
                                    sy.wait_ge(s_te, st.get('jit_te', 0))
                                    for sem_, val in ld_guards:
                                        sy.wait_ge(sem_, val)
                                SW[slot].b(sy.dma_start(
                                    out=wjit[:, slot * 512:(slot + 1) * 512],
                                    in_=wsrc_fn(grp, ch)))
                                st[(tag, th, 'ld', grp, ch)] = SW[slot].v
                            block.sync(w_ld)

                            def w_mm(t, grp=grp, ch=ch, slot=slot, tag=tag):
                                if grp == 0 and ch == 0 and mm_wait:
                                    for sem_, val in mm_wait():
                                        t.wait_ge(sem_, val)
                                if ch == 0 and (tag, th, 'ev', grp - 1) in st:
                                    ev, ea = st[(tag, th, 'ev', grp - 1)]
                                    t.wait_ge(s_ve, ev)
                                    t.wait_ge(s_ae, ea)
                                t.wait_ge(SWs[slot], st[(tag, th, 'ld', grp, ch)])
                                for q in range(4):
                                    TE.b(t.matmul(
                                        pb[q][:, :],
                                        wjit[:, slot * 512 + q * 128:
                                             slot * 512 + (q + 1) * 128],
                                        rhs_fn(ch),
                                        start=(ch == 0), stop=(ch == nChunk - 1)))
                                st[(tag, th, 'mm', grp, ch)] = TE.v
                            block.tensor(w_mm)
                        evac_fn(grp)
                    st['jit_te'] = TE.v

                def g_evac(grp, th=th):
                    def f(sc, grp=grp):
                        sc.wait_ge(s_te, st[('g', th, 'mm', grp, NCH - 1)])
                        for q in range(4):
                            of = grp * 4 + q
                            AE.b(sc.activation(gateT[:, of * 512:(of + 1) * 512],
                                               pb[q][:, :], AF.Sigmoid))
                        st[('g', th, 'ev', grp)] = (VE.v, AE.v)
                        st[('g_done', th, grp)] = AE.v
                    block.scalar(f)

                jit_mm('g', lambda grp, ch: wg_b.ap()[ch, :, grp * 512:(grp + 1) * 512],
                       4, NCH, lambda ch: xnoB[:, ch * 512:(ch + 1) * 512],
                       g_evac, mm_wait=lambda: [(s_ve, st[('b_done', th)])],
                       ld_guards=((s_ds, st['pcast']),))

                for c in range(NCH):
                    def d1_dma(sy, c=c, th=th, t0=t0):
                        g = []
                        if ('d1s', th, c - 2) in st:
                            g.append((s_ae, st[('d1s', th, c - 2)]))
                        if th == 0 and c == 0:
                            g.append((s_cc, st['ag_a0']))
                        if c < 2:
                            g.append((s_ve, st[('b_xq', th, 14 + c)]))
                        st[('d1l', th, c)] = tqs_br(
                            sy, at0[c % 2][:, :],
                            lambda gg, c=c, t0=t0: aga_o.ap()[
                                c * 128:(c + 1) * 128,
                                gg * TOWN + t0:gg * TOWN + t0 + 512],
                            c % 2, g)
                    block.sync(d1_dma)

                    def d1_sq(sc, c=c, th=th):
                        sc.wait_ge(SXs[c % 2], st[('d1l', th, c)])
                        if ('d1mm', th, c - 2) in st:
                            sc.wait_ge(s_te, st[('d1mm', th, c - 2)])
                        AE.b(sc.activation(sqb2[:, (c % 2) * 512:(c % 2 + 1) * 512],
                                           at0[c % 2][:, :], AF.Square))
                        st[('d1s', th, c)] = AE.v
                    block.scalar(d1_sq)

                    def d1_mm(t, c=c, th=th):
                        t.wait_ge(s_ae, st[('d1s', th, c)])
                        if c == 0 and th > 0:
                            t.wait_ge(s_ve, st[('d_row', 0)])
                        q2 = sqb2[:, (c % 2) * 512:(c % 2 + 1) * 512]
                        for hh in range(2):
                            TE.b(t.matmul(pb5[0:1, hh * 256:hh * 256 + 256],
                                          ones_bf[:, :],
                                          q2[:, hh * 256:hh * 256 + 256],
                                          start=(c == 0 and hh == 0),
                                          stop=(c == NCH - 1 and hh == 1)))
                        st[('d1mm', th, c)] = TE.v
                    block.tensor(d1_mm)

                def d_row(v, th=th):
                    v.wait_ge(s_te, st[('d1mm', th, NCH - 1)])
                    VE.b(v.tensor_scalar_mul(rows[:, 0:512], pb5[0:1, :], 1.0 / D))
                    st[('d_rowa', th)] = VE.v
                block.vector(d_row)

                def d_sd(sc, th=th):
                    sc.wait_ge(s_ve, st[('d_rowa', th)])
                    AE.b(sc.activation(rows[:, 0:512], rows[:, 0:512], AF.Sqrt,
                                       bias=cst[0:1, 1:2]))
                    st[('d_sd', th)] = AE.v
                block.scalar(d_sd)

                def d_iv(v, th=th):
                    v.wait_ge(s_ae, st[('d_sd', th)])
                    VE.b(v.reciprocal(rows[:, 512:1024], rows[:, 0:512]))
                    st[('d_row', th)] = VE.v
                block.vector(d_iv)

                def d_rep(sc, th=th, t0=t0):
                    sc.wait_ge(s_ve, st[('d_row', th)])
                    if th > 0:
                        sc.wait_ge(s_ve, st[('y', 0, NCH - 1)])
                    AQ.b(sc.dma_start(out=drep.ap()[9:10, t0:t0 + 512],
                                      in_=rows[:, 512:1024]))
                    sc.wait_ge(s_aq, AQ.v)
                    AQ.b(sc.dma_start(
                        out=r_rp[:, :],
                        in_=drep.ap()[9:10, t0:t0 + 512].partition_broadcast(128)))
                    sc.wait_ge(s_aq, AQ.v)
                    sc.sem_inc(s_af, 1)
                    AFF.v += 1
                    st[('d_rep', th)] = AFF.v
                block.scalar(d_rep)

                for c in range(NCH):
                    def d2_dma(sy, c=c, th=th, t0=t0):
                        g = []
                        if ('y', th, c - 2) in st:
                            g.append((s_ve, st[('y', th, c - 2)]))
                        if c < 2:
                            g.append((s_ae, st[('d1s', th, 14 + c)]))
                        st[('d2l', th, c)] = tqs_br(
                            sy, at0[c % 2][:, :],
                            lambda gg, c=c, t0=t0: aga_o.ap()[
                                c * 128:(c + 1) * 128,
                                gg * TOWN + t0:gg * TOWN + t0 + 512],
                            c % 2, g)
                    block.sync(d2_dma)

                    def d2_v(v, c=c, th=th):
                        v.wait_ge(SXs[c % 2], st[('d2l', th, c)])
                        v.wait_ge(s_af, st[('d_rep', th)])
                        v.wait_ge(s_ae, st[('g_done', th, 3)])
                        if ('y', th, c - 2) in st:
                            v.wait_ge(s_ve, st[('y', th, c - 2)])
                        s1 = scrF[c % 2][:, :]
                        VE.b(v.scalar_tensor_tensor(s1, at0[c % 2][:, :],
                                                    bcols[:, c:c + 1], r_rp[:, :],
                                                    ALU.mult, ALU.mult))
                        v.wait_ge(s_ve, VE.v)
                        gsl = gateT[:, c * 512:(c + 1) * 512]
                        VE.b(v.tensor_mul(gsl, s1, gsl))
                        st[('y', th, c)] = VE.v
                    block.vector(d2_v)
                st[('y_done', th)] = VE.v

                def o_evac(grp, th=th, t0=t0):
                    for q in range(4):
                        oc = grp * 4 + q

                        def f_d(sy, oc=oc, th=th, t0=t0):
                            g = []
                            if ('x1e', th, oc - 2) in st:
                                g.append((s_ve, st[('x1e', th, oc - 2)]))
                            if oc < 2:
                                g.append((s_ve, st[('y', th, 14 + oc)]))
                            st[('e_ld', th, oc)] = tqs(
                                sy, xto[oc % 2][:, :],
                                xT_own[oc * 128:(oc + 1) * 128, t0:t0 + 512],
                                oc % 2, g)
                        block.sync(f_d)

                        def f_v(v, oc=oc, q=q, th=th, grp=grp):
                            v.wait_ge(s_te, st[('o', th, 'mm', grp, NCH - 1)])
                            v.wait_ge(SXs[oc % 2], st[('e_ld', th, oc)])
                            if ('x1e', th, oc - 2) in st:
                                v.wait_ge(s_ve, st[('x1e', th, oc - 2)])
                            VE.b(v.tensor_add(x1h[:, oc * 512:(oc + 1) * 512],
                                              pb[q][:, :], xto[oc % 2][:, :]))
                            st[('x1e', th, oc)] = VE.v
                            if q == 3:
                                st[('o', th, 'ev', grp)] = (VE.v, 0)
                        block.vector(f_v)

                jit_mm('o', lambda grp, ch: wo_b.ap()[ch, :, grp * 512:(grp + 1) * 512],
                       4, NCH, lambda ch: gateT[:, ch * 512:(ch + 1) * 512],
                       o_evac, mm_wait=lambda: [(s_ve, st[('y_done', th)])],
                       ld_guards=((s_ds, st['pcast']),))

                arB.pop()
                hT = arB.alloc(64 * 512)
                wjit = arB.alloc(4 * 512)

                def f1_evac(grp, th=th):
                    for q in range(4):
                        ft = grp * 4 + q

                        def f_z(v, grp=grp, q=q, ft=ft):
                            if q == 0:
                                v.wait_ge(s_te, st[('m1', th, 'mm', grp, NCH - 1)])
                                v.wait_ge(s_af, st[('a_rep', th)])
                            if ('sil', th, ft - 2) in st:
                                v.wait_ge(s_ae, st[('sil', th, ft - 2)])
                            VE.b(v.tensor_mul(scrF[q % 2][:, :], pb[q][:, :],
                                              i1_rp[:, :]))
                            st[('z', th, ft)] = VE.v
                            if q == 3:
                                st[('m1', th, 'ev', grp)] = (VE.v, 0)
                        block.vector(f_z)

                        def f_a(sc, q=q, ft=ft):
                            sc.wait_ge(s_ve, st[('z', th, ft)])
                            AE.b(sc.activation(hT[:, ft * 512:(ft + 1) * 512],
                                               scrF[q % 2][:, :], AF.Silu))
                            st[('sil', th, ft)] = AE.v
                        block.scalar(f_a)

                        def f_t(v, ft=ft):
                            v.wait_ge(s_ae, st[('sil', th, ft)])
                            if ft == 0:
                                VE.b(v.memset(acc[:, :], 0.0))
                                VE.b(v.memset(accn[:, :], 0.0))
                            v.wait_ge(s_ve, VE.v)
                            VE.b(v.tensor_max(acc[:, :],
                                              hT[:, ft * 512:(ft + 1) * 512],
                                              acc[:, :]))
                            v.wait_ge(s_ve, VE.v)
                            VE.b(v.tensor_tensor(accn[:, :],
                                                 hT[:, ft * 512:(ft + 1) * 512],
                                                 accn[:, :], ALU.min))
                            st[('tree', th, ft)] = VE.v
                        block.vector(f_t)

                def m1_src(grp, ch):
                    rb = (4 * grp) // 8
                    fi = (4 * grp) % 8
                    return agw_o.ap()[rb * 4096 + ch * 128:rb * 4096 + (ch + 1) * 128,
                                      fi * 128:fi * 128 + 512]

                jit_mm('m1', m1_src, 16, NCH,
                       lambda ch: xqT[:, ch * 512:(ch + 1) * 512],
                       f1_evac, mm_wait=lambda: [(s_ve, st[('b_done', th)]),
                                                 (s_cc, st['ag_w']),
                                                 (s_ve, st[('x1e', th, 15)])],
                       ld_guards=((s_cc, st['ag_w']),))

                def f2_neg(v, th=th):
                    v.wait_ge(s_ve, st[('tree', th, 63)])
                    VE.b(v.tensor_scalar_mul(accn[:, :], accn[:, :], -1.0))
                    st[('negn', th)] = VE.v
                block.vector(f2_neg)

                def f2_redc(g, th=th):
                    g.wait_ge(s_ve, st[('negn', th)])
                    GE.b(g.tensor_reduce(rows[0:1, 0:512], acc[:, :], axis=AXC.C,
                                         op=ALU.max))
                    GE.b(g.tensor_reduce(rows[0:1, 512:1024], accn[:, :],
                                         axis=AXC.C, op=ALU.max))
                    st[('hmax', th)] = GE.v
                block.gpsimd(f2_redc)

                def f2_rows(v, th=th):
                    v.wait_ge(s_ge, st[('hmax', th)])
                    VE.b(v.tensor_max(rows[:, 0:512], rows[:, 0:512],
                                      rows[:, 512:1024]))
                    v.wait_ge(s_ve, VE.v)
                    VE.b(v.tensor_scalar_max(rows[:, 0:512], rows[:, 0:512], 1e-5))
                    v.wait_ge(s_ve, VE.v)
                    VE.b(v.reciprocal(rows[:, 512:1024], rows[:, 0:512]))
                    v.wait_ge(s_ve, VE.v)
                    VE.b(v.tensor_scalar_mul(rows[:, 512:1024], rows[:, 512:1024],
                                             127.0))
                    VE.b(v.tensor_scalar(rows[:, 0:512], rows[:, 0:512],
                                         wrow[0:1, 1:2], 1.0 / 127.0,
                                         ALU.mult, ALU.mult))
                    st[('f2r', th)] = VE.v
                block.vector(f2_rows)

                def f2_rep(sc, th=th, t0=t0):
                    sc.wait_ge(s_ve, st[('f2r', th)])
                    AQ.b(sc.dma_start(out=drep.ap()[10:11, t0:t0 + 512],
                                      in_=rows[:, 512:1024]))
                    AQ.b(sc.dma_start(out=drep.ap()[11:12, t0:t0 + 512],
                                      in_=rows[:, 0:512]))
                    sc.wait_ge(s_aq, AQ.v)
                    AQ.b(sc.dma_start(out=s2_rp[:, :],
                                      in_=drep.ap()[10:11, t0:t0 + 512]
                                      .partition_broadcast(128)))
                    AQ.b(sc.dma_start(out=i2_rp[:, :],
                                      in_=drep.ap()[11:12, t0:t0 + 512]
                                      .partition_broadcast(128)))
                    sc.wait_ge(s_aq, AQ.v)
                    sc.sem_inc(s_af, 1)
                    AFF.v += 1
                    st[('f2rep', th)] = AFF.v
                block.scalar(f2_rep)

                for ft in range(64):
                    def fq(v, ft=ft, th=th):
                        if ft == 0:
                            v.wait_ge(s_af, st[('f2rep', th)])
                        hsl = hT[:, ft * 512:(ft + 1) * 512]
                        s1 = scrF[ft % 2][:, :]
                        if ('fq', th, ft - 2) in st:
                            v.wait_ge(s_ve, st[('fq', th, ft - 2)])
                        VE.b(v.tensor_mul(s1, hsl, s2_rp[:, :]))
                        v.wait_ge(s_ve, VE.v)
                        VE.b(v.tensor_scalar(s1, s1, MAGIC, MAGIC, ALU.add,
                                             ALU.subtract))
                        v.wait_ge(s_ve, VE.v)
                        VE.b(v.tensor_scalar(hsl, s1, 127.0, -128.0,
                                             ALU.min, ALU.max))
                        st[('fq', th, ft)] = VE.v
                    block.vector(fq)
                st[('fq_done', th)] = VE.v

                def m2_src(grp, ch):
                    rb, pi = ch // 8, ch % 8
                    base = rb * 4096 + 2048 + pi * 256
                    return agw_o.ap()[base:base + 256, :].rearrange(
                        "(p a) f -> p (a f)", a=2)[:, grp * 512:(grp + 1) * 512]

                def m2_evac(grp, th=th, t0=t0):
                    def f_v(v, grp=grp):
                        v.wait_ge(s_te, st[('m2', th, 'mm', grp, 63)])
                        for q in range(4):
                            oc = grp * 4 + q
                            s1 = scrF[q % 2][:, :]
                            if q >= 2:
                                v.wait_ge(s_ve, VE.v)
                            VE.b(v.tensor_mul(s1, pb[q][:, :], i2_rp[:, :]))
                            v.wait_ge(s_ve, VE.v)
                            VE.b(v.tensor_add(x1h[:, oc * 512:(oc + 1) * 512],
                                              x1h[:, oc * 512:(oc + 1) * 512], s1))
                            st[('x1f', th, oc)] = VE.v
                            if q == 3:
                                st[('m2', th, 'ev', grp)] = (VE.v, 0)
                    block.vector(f_v)

                    def f_d(sy, grp=grp):
                        for q in range(4):
                            oc = grp * 4 + q
                            sy.wait_ge(s_ve, st[('x1f', th, oc)])
                            TQ.b(sy.dma_start(
                                out=x1d.ap()[oc * 128:(oc + 1) * 128, t0:t0 + 512],
                                in_=x1h[:, oc * 512:(oc + 1) * 512]))
                    block.sync(f_d)

                jit_mm('m2', m2_src, 4, 64,
                       lambda ch: hT[:, ch * 512:(ch + 1) * 512],
                       m2_evac, mm_wait=lambda: [(s_ve, st[('fq_done', th)])],
                       ld_guards=((s_cc, st['ag_w']),))

                arB.pop()

                def x1_fence(sy, th=th):
                    sy.wait_ge(s_tq, TQ.v)
                    sy.sem_inc(s_ds, 1)
                    DS.v += 1
                    st[('x1dma', th)] = DS.v
                block.sync(x1_fence)

            st['p3_ve'] = VE.v
            arF.pop()
            arB.pop()

        if BUILD_STAGE >= 4:
            ent4_ve = VE.v
            ent4_ds = DS.v
            ent4_aq = AQ.v
            arF.push()
            arB.push()
            xn1T = arB.alloc(NCH * 1024)
            attn1T = arB.alloc(NCH * 1024)
            wjit4 = arB.alloc(4 * 512)
            scr4 = arF.alloc(2 * 512)
            fs_rep = arF.alloc(NTRIP)
            fb_rep = arF.alloc(NTRIP)
            m4 = arF.alloc(512)
            i4 = arF.alloc(512)

            def p4_rep0(sc):
                sc.wait_ge(s_ve, ent4_ve)
                sc.wait_ge(s_ds, ent4_ds)
                AQ.b(sc.dma_start(out=fs_rep[:, :],
                                  in_=fsel[0:1, :].partition_broadcast(128)))
                AQ.b(sc.dma_start(out=fb_rep[:, :],
                                  in_=fbias[0:1, :].partition_broadcast(128)))
                sc.wait_ge(s_aq, AQ.v)
                sc.sem_inc(s_af, 1)
                AFF.v += 1
            block.scalar(p4_rep0)
            st['fsb'] = AFF.v

            arF.push()
            arB.push()
            x1s = arF.alloc(NCH * 512)
            tsc4 = arF.alloc(512)
            xb4 = arB.alloc(2 * 512)
            sq4 = arB.alloc(2 * 512)

            for th in range(2):
                t0 = th * 512
                for c in range(NCH):
                    def l_ld(sy, c=c, th=th, t0=t0):
                        g = []
                        if th == 0 and c < 2:
                            g.append((s_ds, ent4_ds))
                            g.append((s_ve, st[('x1e', 1, 14 + c)]))
                        if ('xb4', th, c - 2) in st:
                            g.append((s_ve, st[('xb4', th, c - 2)]))
                        if th == 1:
                            g.append((s_ve, st[('xn1c', 0, c)]))
                        st[('l_ld', th, c)] = tqs(
                            sy, x1s[:, c * 512:(c + 1) * 512],
                            x1d.ap()[c * 128:(c + 1) * 128, t0:t0 + 512],
                            c % 2, g)
                    block.sync(l_ld)

                    def l_cp(v, c=c, th=th):
                        v.wait_ge(SXs[c % 2], st[('l_ld', th, c)])
                        k = (th, c - 2) if c >= 2 else ((th - 1, 14 + c)
                                                        if th > 0 else None)
                        if k:
                            v.wait_ge(s_ae, st[('sq4',) + k])
                            v.wait_ge(s_te, st[('smm4',) + k])
                        VE.b(v.tensor_copy(xb4[:, (c % 2) * 512:(c % 2 + 1) * 512],
                                           x1s[:, c * 512:(c + 1) * 512]))
                        st[('xb4', th, c)] = VE.v
                    block.vector(l_cp)

                    def l_sq(sc, c=c, th=th):
                        sc.wait_ge(s_ve, st[('xb4', th, c)])
                        k = (th, c - 2) if c >= 2 else ((th - 1, 14 + c)
                                                        if th > 0 else None)
                        if k:
                            sc.wait_ge(s_te, st[('smm4',) + k])
                        AE.b(sc.activation(sq4[:, (c % 2) * 512:(c % 2 + 1) * 512],
                                           xb4[:, (c % 2) * 512:(c % 2 + 1) * 512],
                                           AF.Square))
                        st[('sq4', th, c)] = AE.v
                    block.scalar(l_sq)

                    def l_mm(t, c=c, th=th):
                        t.wait_ge(s_ae, st[('sq4', th, c)])
                        if th == 0 and c == 0:
                            t.wait_ge(s_ve, ent4_ve)
                        if th == 1 and c == 0:
                            t.wait_ge(s_ve, st[('var4', 0)])
                        x2 = xb4[:, (c % 2) * 512:(c % 2 + 1) * 512]
                        q2 = sq4[:, (c % 2) * 512:(c % 2 + 1) * 512]
                        for hh in range(2):
                            TE.b(t.matmul(pb5[0:1, hh * 256:hh * 256 + 256],
                                          ones_bf[:, :],
                                          x2[:, hh * 256:hh * 256 + 256],
                                          start=(c == 0 and hh == 0),
                                          stop=(c == NCH - 1 and hh == 1)))
                        for hh in range(2):
                            TE.b(t.matmul(pb4[0:1, hh * 256:hh * 256 + 256],
                                          ones_bf[:, :],
                                          q2[:, hh * 256:hh * 256 + 256],
                                          start=(c == 0 and hh == 0),
                                          stop=(c == NCH - 1 and hh == 1)))
                        st[('smm4', th, c)] = TE.v
                    block.tensor(l_mm)

                def l_rows(v, th=th):
                    v.wait_ge(s_te, st[('smm4', th, NCH - 1)])
                    if th == 0:
                        v.wait_ge(s_aq, ent4_aq)
                    else:
                        v.wait_ge(s_aq, st[('repq4', 0)])
                        v.wait_ge(s_ae, st[('sd4', 0)])
                    VE.b(v.tensor_scalar_mul(rows[:, R_M:R_M + 512],
                                             pb5[0:1, :], 1.0 / D))
                    VE.b(v.tensor_scalar_mul(rows[:, R_S1:R_S1 + 512],
                                             pb4[0:1, :], 1.0 / D))
                    v.wait_ge(s_ve, VE.v)
                    VE.b(v.tensor_mul(rows[:, R_S2:R_S2 + 512],
                                      rows[:, R_M:R_M + 512],
                                      rows[:, R_M:R_M + 512]))
                    v.wait_ge(s_ve, VE.v)
                    VE.b(v.tensor_sub(rows[:, R_S2:R_S2 + 512],
                                      rows[:, R_S1:R_S1 + 512],
                                      rows[:, R_S2:R_S2 + 512]))
                    st[('var4', th)] = VE.v
                block.vector(l_rows)

                def l_sd(sc, th=th):
                    sc.wait_ge(s_ve, st[('var4', th)])
                    AE.b(sc.activation(rows[:, R_S2:R_S2 + 512],
                                       rows[:, R_S2:R_S2 + 512], AF.Sqrt,
                                       bias=cst[0:1, 0:1]))
                    st[('sd4', th)] = AE.v
                block.scalar(l_sd)

                def l_iv(v, th=th):
                    v.wait_ge(s_ae, st[('sd4', th)])
                    VE.b(v.reciprocal(rows[:, R_IV:R_IV + 512],
                                      rows[:, R_S2:R_S2 + 512]))
                    st[('iv4', th)] = VE.v
                block.vector(l_iv)

                def l_rep(sc, th=th, t0=t0):
                    sc.wait_ge(s_ve, st[('iv4', th)])
                    if th == 1:
                        sc.wait_ge(s_ve, st[('xn1c', 0, NCH - 1)])
                    AQ.b(sc.dma_start(out=drep.ap()[13:14, t0:t0 + 512],
                                      in_=rows[:, R_M:R_M + 512]))
                    AQ.b(sc.dma_start(out=drep.ap()[14:15, t0:t0 + 512],
                                      in_=rows[:, R_IV:R_IV + 512]))
                    st[('repq4', th)] = AQ.v
                    sc.wait_ge(s_aq, AQ.v)
                    AQ.b(sc.dma_start(
                        out=m4[:, :],
                        in_=drep.ap()[13:14, t0:t0 + 512].partition_broadcast(128)))
                    AQ.b(sc.dma_start(
                        out=i4[:, :],
                        in_=drep.ap()[14:15, t0:t0 + 512].partition_broadcast(128)))
                    sc.wait_ge(s_aq, AQ.v)
                    sc.sem_inc(s_af, 1)
                    AFF.v += 1
                    st[('rep4', th)] = AFF.v
                block.scalar(l_rep)

                for c in range(NCH):
                    def l_xn(v, c=c, th=th, t0=t0):
                        if c == 0:
                            v.wait_ge(s_af, st[('rep4', th)])
                        v.wait_ge(s_ve, VE.v)
                        VE.b(v.tensor_sub(tsc4[:, :],
                                          x1s[:, c * 512:(c + 1) * 512],
                                          m4[:, :]))
                        v.wait_ge(s_ve, VE.v)
                        VE.b(v.tensor_mul(
                            xn1T[:, c * 1024 + t0:c * 1024 + t0 + 512],
                            tsc4[:, :], i4[:, :]))
                        st[('xn1c', th, c)] = VE.v
                    block.vector(l_xn)

            arF.pop()
            arB.pop()

            def jit4(tag, th, wsrc_fn, nGrp, nChunk, rhs_fn, evac_fn,
                     mm_wait=None, ld_guards=()):
                for grp in range(nGrp):
                    for ch in range(nChunk):
                        slot = (grp * nChunk + ch) % 4

                        def w_ld(sy, grp=grp, ch=ch, slot=slot, tag=tag, th=th):
                            k = (tag, th, 'mm', grp, ch - 4) if ch >= 4 else \
                                ((tag, th, 'mm', grp - 1, nChunk - 4 + ch)
                                 if grp > 0 else None)
                            if k and k in st:
                                sy.wait_ge(s_te, st[k])
                            else:
                                sy.wait_ge(s_te, st.get('jit_te', 0))
                                for sem_, val in ld_guards:
                                    sy.wait_ge(sem_, val)
                            SW[slot].b(sy.dma_start(
                                out=wjit4[:, slot * 512:(slot + 1) * 512],
                                in_=wsrc_fn(grp, ch)))
                            st[(tag, th, 'ld', grp, ch)] = SW[slot].v
                        block.sync(w_ld)

                        def w_mm(t, grp=grp, ch=ch, slot=slot, tag=tag, th=th):
                            if grp == 0 and ch == 0 and mm_wait:
                                for sem_, val in mm_wait():
                                    t.wait_ge(sem_, val)
                            if ch == 0 and (tag, th, 'ev', grp - 1) in st:
                                ev, ea = st[(tag, th, 'ev', grp - 1)]
                                t.wait_ge(s_ve, ev)
                                if ea:
                                    t.wait_ge(s_ae, ea)
                            t.wait_ge(SWs[slot], st[(tag, th, 'ld', grp, ch)])
                            for q in range(4):
                                TE.b(t.matmul(
                                    pb[q][:, :],
                                    wjit4[:, slot * 512 + q * 128:
                                          slot * 512 + (q + 1) * 128],
                                    rhs_fn(ch),
                                    start=(ch == 0), stop=(ch == nChunk - 1)))
                            st[(tag, th, 'mm', grp, ch)] = TE.v
                        block.tensor(w_mm)
                    evac_fn(tag, th, grp)
                st['jit_te'] = TE.v

            arF.push()
            arB.push()
            QT = arB.alloc(NCH * 1024)
            kstg = arB.alloc(4 * 512)
            vtstg = arB.alloc(2 * 512)
            vstg = arB.alloc(2 * 512)
            kth = arB.alloc(2 * 4096)
            vth = arB.alloc(2 * 4096)
            PT2 = arB.alloc(2 * 512)
            rp2 = arF.alloc(2 * 256)

            def ev_q(tag, th, grp):
                def f(v, th=th, grp=grp):
                    v.wait_ge(s_te, st[('q4', th, 'mm', grp, NCH - 1)])
                    for q in range(4):
                        of = grp * 4 + q
                        VE.b(v.tensor_scalar_add(
                            QT[:, of * 1024 + th * 512:of * 1024 + th * 512 + 512],
                            pb[q][:, :], bcols[:, NCH + of:NCH + of + 1]))
                    st[('q4', th, 'ev', grp)] = (VE.v, 0)
                block.vector(f)

            def ev_k(tag, th, grp):
                def f(v, th=th, grp=grp):
                    v.wait_ge(s_te, st[('k4', th, 'mm', grp, NCH - 1)])
                    kprev = ('k4d', th, grp - 1) if grp > 0 else \
                        (('k4d', 0, 3) if th == 1 else None)
                    if kprev:
                        v.wait_ge(s_tq, st[kprev])
                    for q in range(4):
                        of = grp * 4 + q
                        VE.b(v.tensor_scalar_add(
                            kstg[:, q * 512:(q + 1) * 512], pb[q][:, :],
                            bcols[:, 2 * NCH + of:2 * NCH + of + 1]))
                    st[('k4e', th, grp)] = VE.v
                    st[('k4', th, 'ev', grp)] = (VE.v, 0)
                block.vector(f)

                def fd(sy, th=th, grp=grp):
                    sy.wait_ge(s_ve, st[('k4e', th, grp)])
                    for q in range(4):
                        of = grp * 4 + q
                        dst = agkv_i.ap()[of * 64:of * 64 + 64, :].rearrange(
                            "a (two u) -> (a two) u", two=2)[:,
                            th * 512:th * 512 + 512]
                        TQ.b(sy.dma_start(out=dst,
                                          in_=kstg[:, q * 512:(q + 1) * 512]))
                    st[('k4d', th, grp)] = TQ.v
                block.sync(fd)

            def ev_v(tag, th, grp):
                for q in range(4):
                    of = grp * 4 + q
                    vv = th * 16 + grp * 4 + q
                    ptb = pt0 if vv % 2 == 0 else pt1

                    def fv(v, q=q, of=of, th=th, grp=grp, vv=vv):
                        if q == 0:
                            v.wait_ge(s_te, st[('v4', th, 'mm', grp, NCH - 1)])
                        if ('v4t', vv - 2) in st:
                            v.wait_ge(s_te, st[('v4t', vv - 2)])
                        VE.b(v.tensor_scalar_add(
                            vtstg[:, (vv % 2) * 512:(vv % 2 + 1) * 512],
                            pb[q][:, :],
                            bcols[:, 5 * NCH + of:5 * NCH + of + 1]))
                        st[('v4e', vv)] = VE.v
                        if q == 3:
                            st[('v4', th, 'ev', grp)] = (VE.v, 0)
                    block.vector(fv)

                    def ft(t, vv=vv, ptb=ptb):
                        t.wait_ge(s_ve, st[('v4e', vv)])
                        if ('v4c', vv - 2) in st:
                            t.wait_ge(s_ve, st[('v4c', vv - 2)])
                        else:
                            t.wait_ge(s_ve, ent4_ve)
                        for k in range(4):
                            TE.b(t.transpose(
                                ptb[:, k * 128:(k + 1) * 128],
                                vtstg[:, (vv % 2) * 512 + k * 128:
                                      (vv % 2) * 512 + (k + 1) * 128],
                                id_bf[:, :]))
                        st[('v4t', vv)] = TE.v
                    block.tensor(ft)

                    def fc(v, vv=vv, ptb=ptb):
                        v.wait_ge(s_te, st[('v4t', vv)])
                        if ('v4d', vv - 1) in st:
                            v.wait_ge(s_tq, st[('v4d', vv - 1)])
                        VE.b(v.tensor_copy(
                            vstg[:, (vv % 2) * 512:(vv % 2 + 1) * 512],
                            ptb[:, 0:512]))
                        st[('v4c', vv)] = VE.v
                    block.vector(fc)

                    def fd(sy, of=of, th=th, vv=vv):
                        sy.wait_ge(s_ve, st[('v4c', vv)])
                        dst = agkv_i.ap()[1024 + th * 512:1024 + th * 512 + 512,
                                          of * 128:(of + 1) * 128].rearrange(
                            "(k p) e -> p k e", p=128)
                        src = vstg[:, (vv % 2) * 512:(vv % 2 + 1) * 512]\
                            .rearrange("p (k e) -> p k e", k=4)
                        TQ.b(sy.dma_start(out=dst, in_=src))
                        st[('v4d', vv)] = TQ.v
                    block.sync(fd)

            for th in range(2):
                jit4('q4', th,
                     lambda grp, ch: wq_b.ap()[ch, :, grp * 512:(grp + 1) * 512],
                     4, NCH,
                     lambda ch, th=th: xn1T[:, ch * 1024 + th * 512:
                                            ch * 1024 + th * 512 + 512],
                     ev_q,
                     mm_wait=(lambda th=th:
                              [(s_ve, st[('xn1c', 1, NCH - 1)])] if th == 0
                              else [(s_ve, st[('q4', 0, 'ev', 3)][0])]),
                     ld_guards=((s_gf, st['pcast']),))
            for th in range(2):
                jit4('k4', th,
                     lambda grp, ch: wk_b.ap()[ch, :, grp * 512:(grp + 1) * 512],
                     4, NCH,
                     lambda ch, th=th: xn1T[:, ch * 1024 + th * 512:
                                            ch * 1024 + th * 512 + 512],
                     ev_k,
                     mm_wait=(lambda th=th:
                              [(s_ve, st[('q4', 1, 'ev', 3)][0])] if th == 0
                              else [(s_ve, st[('k4', 0, 'ev', 3)][0])]),
                     ld_guards=((s_gf, st['pcast']),))
            for th in range(2):
                jit4('v4', th,
                     lambda grp, ch: wv_b.ap()[ch, :, grp * 512:(grp + 1) * 512],
                     4, NCH,
                     lambda ch, th=th: xn1T[:, ch * 1024 + th * 512:
                                            ch * 1024 + th * 512 + 512],
                     ev_v,
                     mm_wait=(lambda th=th:
                              [(s_ve, st[('k4', 1, 'ev', 3)][0])] if th == 0
                              else [(s_ve, st[('v4', 0, 'ev', 3)][0])]),
                     ld_guards=((s_gf, st['pcast']),))

            def kv_fence(sy):
                sy.wait_ge(s_tq, TQ.v)
                sy.sem_inc(s_ds, 1)
                DS.v += 1
            block.sync(kv_fence)
            st['kvf'] = DS.v

            def p4_ag(g):
                g.wait_ge(s_ds, st['kvf'])
                CC.b(g.collective_compute(
                    "AllGather", ALU.bypass, replica_groups=GRP4,
                    ins=[agkv_i.ap().opt()], outs=[agkv_o.ap().opt()]))
            block.gpsimd(p4_ag)
            st['ag_kv'] = CC.v

            for h in range(16):
                kq, vq = h % 2, 2 + (h % 2)
                kof = (h % 2) * 4096

                def f_ld(sy, h=h, kq=kq, vq=vq, kof=kof):
                    if h < 2:
                        sy.wait_ge(s_cc, st['ag_kv'])
                        sy.wait_ge(s_te, st['jit_te'])
                    else:
                        sy.wait_ge(s_te, st[('fl_te', h - 2)])
                    for r in range(4):
                        SW[kq].b(sy.dma_start(
                            out=kth[:, kof + r * 1024:kof + (r + 1) * 1024],
                            in_=agkv_o.ap()[r * 2048 + h * 64:
                                            r * 2048 + h * 64 + 64, :]
                            .rearrange("a (two u) -> (a two) u", two=2)))
                    for r in range(4):
                        SW[vq].b(sy.dma_start(
                            out=vth[:, kof + r * 1024:kof + (r + 1) * 1024]
                            .rearrange("p (a e) -> p a e", a=8),
                            in_=agkv_o.ap()[r * 2048 + 1024:r * 2048 + 2048,
                                            h * 128:(h + 1) * 128]
                            .rearrange("(a p) e -> p a e", p=128)))
                    st[('fl_kld', h)] = SW[kq].v
                    st[('fl_vld', h)] = SW[vq].v
                block.sync(f_ld)

                jj = 0
                for qs in range(4):
                    n_t = 4 * (qs + 1)
                    tl = [(ss, r) for ss in range(qs + 1) for r in range(4)]
                    for ti, (ss, r) in enumerate(tl):
                        j = QS_BASE[qs] + ti
                        p2 = jj % 2
                        pbS = pb0 if p2 == 0 else pb1
                        first = (ti == 0)
                        last = (ti == n_t - 1)

                        def t_s(t, h=h, jj=jj, qs=qs, ss=ss, r=r, pbS=pbS,
                                kof=kof):
                            if jj == 0:
                                t.wait_ge(SWs[h % 2], st[('fl_kld', h)])
                                t.wait_ge(SWs[2 + h % 2], st[('fl_vld', h)])
                            k = ('fl_e', h, jj - 2) if jj >= 2 else \
                                (('fl_e', h - 1, 38 + jj) if h > 0 else None)
                            if k:
                                t.wait_ge(s_ae, st[k])
                            for jh in range(2):
                                TE.b(t.matmul(
                                    pbS[:, jh * 256:(jh + 1) * 256],
                                    kth[:, kof + r * 1024 + ss * 256 + jh * 128:
                                         kof + r * 1024 + ss * 256 +
                                         (jh + 1) * 128],
                                    QT[:, h * 1024 + qs * 256:
                                       h * 1024 + (qs + 1) * 256],
                                    start=True, stop=True))
                            st[('fl_s', h, jj)] = TE.v
                        block.tensor(t_s)

                        def t_e(sc, h=h, jj=jj, j=j, p2=p2, pbS=pbS):
                            sc.wait_ge(s_te, st[('fl_s', h, jj)])
                            k = ('fl_pv', h, jj - 2) if jj >= 2 else \
                                (('fl_pv', h - 1, 38 + jj) if h > 0 else None)
                            if k:
                                sc.wait_ge(s_te, st[k])
                            if h == 0 and jj == 0:
                                sc.wait_ge(s_af, st['fsb'])
                            AE.b(sc.activation(
                                PT2[:, p2 * 512:(p2 + 1) * 512], pbS[:, :],
                                AF.Exp, scale=QK_SCALE,
                                bias=fb_rep[:, j:j + 1]))
                            st[('fl_e', h, jj)] = AE.v
                        block.scalar(t_e)

                        def t_m(v, h=h, jj=jj, j=j, p2=p2):
                            v.wait_ge(s_ae, st[('fl_e', h, jj)])
                            VE.b(v.scalar_tensor_tensor(
                                PT2[:, p2 * 512:(p2 + 1) * 512], cmask[:, :],
                                fs_rep[:, j:j + 1],
                                PT2[:, p2 * 512:(p2 + 1) * 512],
                                ALU.max, ALU.mult))
                            st[('fl_m', h, jj)] = VE.v
                        block.vector(t_m)

                        def t_pv(t, h=h, jj=jj, qs=qs, ss=ss, r=r, p2=p2,
                                 first=first, last=last, kof=kof):
                            t.wait_ge(s_ve, st[('fl_m', h, jj)])
                            if first:
                                k = ('fl_oev', h, qs - 2) if qs >= 2 else \
                                    (('fl_oev', h - 1, qs + 2) if h > 0 else None)
                                if k:
                                    t.wait_ge(s_ve, st[k])
                            pbo = pb2 if qs % 2 == 0 else pb3
                            pbsum = pb4 if qs % 2 == 0 else pb5
                            for jh in range(2):
                                TE.b(t.matmul(
                                    pbo[:, 0:256],
                                    vth[:, kof + r * 1024 + (ss * 2 + jh) * 128:
                                         kof + r * 1024 + (ss * 2 + jh + 1) * 128],
                                    PT2[:, p2 * 512 + jh * 256:
                                         p2 * 512 + (jh + 1) * 256],
                                    start=(first and jh == 0),
                                    stop=(last and jh == 1)))
                            for jh in range(2):
                                TE.b(t.matmul(
                                    pbsum[0:1, 0:256], ones_bf[:, :],
                                    PT2[:, p2 * 512 + jh * 256:
                                         p2 * 512 + (jh + 1) * 256],
                                    start=(first and jh == 0),
                                    stop=(last and jh == 1)))
                            st[('fl_pv', h, jj)] = TE.v
                        block.tensor(t_pv)
                        jj += 1

                    st[('fl_last', h, qs)] = TE.v
                    sl = (h * 4 + qs) % 8

                    def t_rc(v, h=h, qs=qs, sl=sl):
                        v.wait_ge(s_te, st[('fl_last', h, qs)])
                        if h == 0 and qs == 0:
                            v.wait_ge(s_af, st[('rep4', 1)])
                        if ('fl_bw', h * 4 + qs - 8) in st:
                            v.wait_ge(s_aq, st[('fl_bw', h * 4 + qs - 8)])
                        pbsum = pb4 if qs % 2 == 0 else pb5
                        VE.b(v.reciprocal(rows[0:1, sl * 256:(sl + 1) * 256],
                                          pbsum[0:1, 0:256]))
                        st[('fl_r', h, qs)] = VE.v
                    block.vector(t_rc)

                    def t_bn(sc, h=h, qs=qs, sl=sl):
                        sc.wait_ge(s_ve, st[('fl_r', h, qs)])
                        k = ('fl_oev', h, qs - 1) if qs > 0 else \
                            (('fl_oev', h - 1, 3) if h > 0 else None)
                        if k:
                            sc.wait_ge(s_ve, st[k])
                        AQ.b(sc.dma_start(
                            out=drep.ap()[15:16, sl * 256:(sl + 1) * 256],
                            in_=rows[0:1, sl * 256:(sl + 1) * 256]))
                        st[('fl_bw', h * 4 + qs)] = AQ.v
                        sc.wait_ge(s_aq, AQ.v)
                        AQ.b(sc.dma_start(
                            out=rp2[:, (qs % 2) * 256:(qs % 2 + 1) * 256],
                            in_=drep.ap()[15:16, sl * 256:(sl + 1) * 256]
                            .partition_broadcast(128)))
                        sc.wait_ge(s_aq, AQ.v)
                        sc.sem_inc(s_af, 1)
                        AFF.v += 1
                        st[('fl_rp', h, qs)] = AFF.v
                    block.scalar(t_bn)

                    def t_ov(v, h=h, qs=qs):
                        v.wait_ge(s_af, st[('fl_rp', h, qs)])
                        pbo = pb2 if qs % 2 == 0 else pb3
                        VE.b(v.tensor_mul(
                            attn1T[:, h * 1024 + qs * 256:
                                   h * 1024 + (qs + 1) * 256],
                            pbo[:, 0:256],
                            rp2[:, (qs % 2) * 256:(qs % 2 + 1) * 256]))
                        st[('fl_oev', h, qs)] = VE.v
                    block.vector(t_ov)

                st[('fl_te', h)] = TE.v
            st['jit_te'] = TE.v

            arB.pop()
            arF.pop()

            arF.push()
            arB.push()
            x2acc = arF.alloc(NCH * 1024)

            def x2_ld(sy):
                sy.wait_ge(s_te, st[('fl_te', 15)])
                sy.wait_ge(s_ve, st[('fl_oev', 15, 3)])
                for c in range(NCH):
                    TQ.b(sy.dma_start(out=x2acc[:, c * 1024:(c + 1) * 1024],
                                      in_=x1d.ap()[c * 128:(c + 1) * 128, :]))
                sy.wait_ge(s_tq, TQ.v)
                sy.sem_inc(s_ds, 1)
                DS.v += 1
            block.sync(x2_ld)
            st['x2f'] = DS.v

            def ev_f(tag, th, grp):
                def f(v, th=th, grp=grp):
                    v.wait_ge(s_te, st[('f4', th, 'mm', grp, NCH - 1)])
                    if th == 0 and grp == 0:
                        v.wait_ge(s_ds, st['x2f'])
                    for q in range(4):
                        oc = grp * 4 + q
                        s1 = scr4[:, (q % 2) * 512:(q % 2 + 1) * 512]
                        v.wait_ge(s_ve, VE.v)
                        VE.b(v.tensor_scalar(
                            s1, pb[q][:, :], FFN1_SCALE,
                            bcols[:, 4 * NCH + oc:4 * NCH + oc + 1],
                            ALU.mult, ALU.add))
                        v.wait_ge(s_ve, VE.v)
                        VE.b(v.tensor_add(
                            x2acc[:, oc * 1024 + th * 512:
                                  oc * 1024 + th * 512 + 512],
                            x2acc[:, oc * 1024 + th * 512:
                                  oc * 1024 + th * 512 + 512], s1))
                    st[('f4', th, 'ev', grp)] = (VE.v, 0)
                block.vector(f)

            for th in range(2):
                jit4('f4', th,
                     lambda grp, ch: fs_b.ap()[ch * 128:(ch + 1) * 128,
                                               grp * 512:(grp + 1) * 512],
                     4, NCH,
                     lambda ch, th=th: xn1T[:, ch * 1024 + th * 512:
                                            ch * 1024 + th * 512 + 512],
                     ev_f,
                     mm_wait=(lambda th=th:
                              [(s_ae, st[('fl_e', 15, 39)]),
                               (s_ve, st[('fl_oev', 15, 3)])] if th == 0
                              else [(s_ve, st[('f4', 0, 'ev', 3)][0])]),
                     ld_guards=((s_gf, st['fs_cast']),))

            def ev_o(tag, th, grp):
                def f(v, th=th, grp=grp):
                    v.wait_ge(s_te, st[('o4', th, 'mm', grp, NCH - 1)])
                    for q in range(4):
                        oc = grp * 4 + q
                        VE.b(v.scalar_tensor_tensor(
                            x2acc[:, oc * 1024 + th * 512:
                                  oc * 1024 + th * 512 + 512],
                            pb[q][:, :],
                            bcols[:, 3 * NCH + oc:3 * NCH + oc + 1],
                            x2acc[:, oc * 1024 + th * 512:
                                  oc * 1024 + th * 512 + 512],
                            ALU.add, ALU.add))
                    st[('o4', th, 'ev', grp)] = (VE.v, 0)
                block.vector(f)

            for th in range(2):
                jit4('o4', th,
                     lambda grp, ch: wo1_b.ap()[ch, :, grp * 512:(grp + 1) * 512],
                     4, NCH,
                     lambda ch, th=th: attn1T[:, ch * 1024 + th * 512:
                                              ch * 1024 + th * 512 + 512],
                     ev_o,
                     mm_wait=(lambda th=th:
                              [(s_ve, st[('f4', 1, 'ev', 3)][0])] if th == 0
                              else [(s_ve, st[('o4', 0, 'ev', 3)][0])]),
                     ld_guards=((s_gf, st['pcast']),))

            def y_out(sy):
                sy.wait_ge(s_ve, st[('o4', 1, 'ev', 3)][0])
                for c in range(NCH):
                    TQ.b(sy.dma_start(out=y[c * 128:(c + 1) * 128, :],
                                      in_=x2acc[:, c * 1024:(c + 1) * 1024]))
                sy.wait_ge(s_tq, TQ.v)
            block.sync(y_out)
            arF.pop()
            arB.pop()

        if BUILD_STAGE < 4:
            def stub_y(sy):
                if BUILD_STAGE >= 3:
                    sy.wait_ge(s_ds, st[('x1dma', 1)])
                    TQ.b(sy.dma_start(out=y[:, :], in_=x1d.ap()[:, :]))
                elif BUILD_STAGE >= 2:
                    sy.wait_ge(s_cc, st['ag_a0'])
                    TQ.b(sy.dma_start(out=y[:, :], in_=aga_o.ap()[:, 0:TOWN]))
                else:
                    sy.wait_ge(s_tq, st['p1_tq'])
                    for r in range(4):
                        TQ.b(sy.dma_start(out=y[r * 512:(r + 1) * 512, :],
                                          in_=aga_i.ap()[:, 0:TOWN]))
                sy.wait_ge(s_tq, TQ.v)
            block.sync(stub_y)

        def fin(g):
            g.wait_ge(s_tq, TQ.v)
            g.wait_ge(s_aq, AQ.v)
            g.wait_ge(s_gq, GQ.v)
            g.wait_ge(s_cc, CC.v)
        block.gpsimd(fin)

    return nc



def _chunked(mT):
    return np.ascontiguousarray(mT.reshape(NCH, 128, mT.shape[1]))


def make_in_maps(inputs):
    x = np.asarray(inputs["L0_attn_hidden_states"], dtype=np.float32)
    slope = np.asarray(inputs["L0_attn_slope_rate"], dtype=np.float32)
    w_qkv = np.asarray(inputs["L0_attn_w_qkv"], dtype=np.float32)
    w_gate = np.asarray(inputs["L0_attn_w_gate"], dtype=np.float32)
    w_out = np.asarray(inputs["L0_attn_w_out"], dtype=np.float32)
    norm_w = np.asarray(inputs["L0_attn_norm_w"], dtype=np.float32)
    W1 = np.asarray(inputs["L0_ffn_W1"], dtype=np.float32)
    W2 = np.asarray(inputs["L0_ffn_W2"], dtype=np.float32)
    Wq = np.asarray(inputs["L1_attn_Wq"], dtype=np.float32)
    bq = np.asarray(inputs["L1_attn_bq"], dtype=np.float32)
    Wk = np.asarray(inputs["L1_attn_Wk"], dtype=np.float32)
    bk = np.asarray(inputs["L1_attn_bk"], dtype=np.float32)
    Wv = np.asarray(inputs["L1_attn_Wv"], dtype=np.float32)
    bvv = np.asarray(inputs["L1_attn_bv"], dtype=np.float32)
    Wo = np.asarray(inputs["L1_attn_Wo"], dtype=np.float32)
    bo = np.asarray(inputs["L1_attn_bo"], dtype=np.float32)
    factors = np.asarray(inputs["L1_ffn_factors"], dtype=np.float32)
    bias1 = np.asarray(inputs["L1_ffn_bias"], dtype=np.float32)

    wqkvT_full = np.ascontiguousarray(w_qkv.T)
    wgateT = _chunked(np.ascontiguousarray(w_gate.T))
    woutT = _chunked(np.ascontiguousarray(w_out.T))
    WqTc = _chunked(np.ascontiguousarray(Wq.T))
    WkTc = _chunked(np.ascontiguousarray(Wk.T))
    WvTc = _chunked(np.ascontiguousarray(Wv.T))
    WoTc = _chunked(np.ascontiguousarray(Wo.T))
    W1T = np.ascontiguousarray(W1.T)
    W2T = np.ascontiguousarray(W2.T)
    bc = np.zeros((128, 6 * NCH), dtype=np.float32)
    for j, t in enumerate((norm_w, bq, bk, bo, bias1, bvv)):
        bc[:, j * NCH:(j + 1) * NCH] = t.reshape(NCH, 128).T

    in_maps = []
    for c in range(8):
        beta, g = c // 4, c % 4
        blocks = own_blocks(g)
        tok = np.concatenate([np.arange(b * BLK, (b + 1) * BLK) for b in blocks])
        xb = x[beta]
        xTb = np.ascontiguousarray(xb.T)
        heads = [4 * g + i for i in range(4)]
        fsel_c = np.ones((1, NTRIP), dtype=np.float32)
        fbias_c = np.zeros((1, NTRIP), dtype=np.float32)
        for jt, (qs, ss, r) in enumerate(TRIPS):
            pq = own_blocks(g)[qs]
            pk = own_blocks(r)[ss]
            if pk == pq:
                fsel_c[0, jt] = 0.0
            elif pk > pq:
                fbias_c[0, jt] = -30000.0
        cols = []
        for which in range(3):
            for h in heads:
                cols.extend(range(which * D + h * DH, which * D + (h + 1) * DH))
        m = {
            "xT": xTb,
            "xT_own": np.ascontiguousarray(xTb[:, tok]),
            "x_own_tm": np.ascontiguousarray(xb[tok, :]),
            "wqkvT": _chunked(np.ascontiguousarray(wqkvT_full[:, cols])),
            "wgateT": wgateT, "woutT": woutT,
            "WqT": WqTc, "WkT": WkTc, "WvT": WvTc,
            "WoT": WoTc,
            "W1Ts": np.ascontiguousarray(W1T[:, c * 1024:(c + 1) * 1024]),
            "W2Ts": np.ascontiguousarray(W2T[c * 1024:(c + 1) * 1024, :]),
            "factor": np.ascontiguousarray(factors[c]),
            "slope4": np.ascontiguousarray(slope[heads].reshape(1, 4)),
            "bcols_h": bc, "bv": bvv.reshape(1, D),
            "fsel": fsel_c, "fbias": fbias_c,
        }
        in_maps.append(m)
    return in_maps


def kernel(**inputs):
    in_maps = make_in_maps(inputs)
    nc = build_program(taps=TAPS)
    res = run_bass_kernel_spmd(nc, in_maps, core_ids=list(range(8)))
    out = np.zeros((B, N, D), dtype=np.float32)
    for c in range(8):
        beta, g = c // 4, c % 4
        blocks = own_blocks(g)
        tok = np.concatenate([np.arange(b * BLK, (b + 1) * BLK) for b in blocks])
        out[beta, tok, :] = res.results[c]["y"].T
    return out


def run_with_taps(inputs, taps):
    in_maps = make_in_maps(inputs)
    nc = build_program(taps=taps)
    res = run_bass_kernel_spmd(nc, in_maps, core_ids=list(range(8)))
    return res.results
